# version 51
# baseline (speedup 1.0000x reference)
import math
import numpy as np
import ml_dtypes
import concourse.bass as bass
import concourse.mybir as mybir
from concourse.bass_utils import run_bass_kernel_spmd
from contextlib import ExitStack

F32 = mybir.dt.float32
BF16 = mybir.dt.bfloat16
AF = mybir.ActivationFunctionType
ALU = mybir.AluOpType
AP = bass.AP
AXX = mybir.AxisListType.X

D = 1024
S = 4096
DQ = 512
DKV = 512
H = 16
HD = 64
DFF = 4096
NDP = D // 128
NQP = DQ // 128
NKP = DKV // 128
NFT = DFF // 128
SCH = 512
NCH = S // SCH
EPS = 1e-5
PI = float(np.pi)
LN1E4 = float(np.log(10000.0))
INV_SCALE = float(np.float32(1.0) / np.float32(np.sqrt(np.float32(H + DKV + DQ))))

STRICT = False
DEBUG_STAGE = None


class Plan:
    ENGINES = ("sync", "tensor", "scalar", "vector", "gpsimd")

    def __init__(self, nc, sems, counts):
        self.nc = nc
        self.sems = sems
        self.count = counts
        self.ops = {e: [] for e in self.ENGINES}

    def emit(self, eng, fn, wait=(), inc=()):
        if STRICT:
            wait = [(s, self.count[s]) for s in self.sems]
        for s, v in wait:
            assert v <= self.count[s], ("wait ahead of plan", s, v, self.count[s])
        incs = []
        for item in inc:
            s, n = item if isinstance(item, tuple) else (item, 1)
            self.count[s] += n
            incs.append((s, n))
        self.ops[eng].append((fn, [(s, int(v)) for s, v in wait], incs))

    def cnt(self, s):
        return self.count[s]

    def run(self, name=None):
        nc = self.nc
        sems = self.sems
        with nc.Block(name) as blk:
            for eng in self.ENGINES:
                if not self.ops[eng]:
                    continue
                ops = self.ops[eng]

                def body(E, ops=ops):
                    hwm = {}
                    for fn, waits, incs in ops:
                        for s, v in waits:
                            if hwm.get(s, -1) < v:
                                E.wait_ge(sems[s], v)
                                hwm[s] = v
                        inst = fn(E)
                        for s, n in incs:
                            inst.then_inc(sems[s], n)
                getattr(blk, eng)(body)
        self.ops = {e: [] for e in self.ENGINES}


def build_program(debug_stage=None):
    nc = bass.Bass("TRN2", target_bir_lowering=False)

    x_in = nc.dram_tensor("x", [S, D], F32, kind="ExternalInput")
    wins = {}
    for nm, shp in [
        ("w1", [DFF, D]), ("b1", [DFF]), ("w2", [D, DFF]), ("b2", [D]),
        ("wdq", [D, DQ]), ("bdq", [DQ]), ("wuq", [DQ, D]), ("buq", [D]),
        ("wqr", [DQ, D]), ("bqr", [D]), ("wkr", [D, D]), ("bkr", [D]),
        ("wdkv", [D, DKV]), ("bdkv", [DKV]), ("wuk", [DKV, D]), ("buk", [D]),
        ("wuv", [DKV, D]), ("buv", [D]), ("wo", [D, D]), ("bo", [D]),
    ]:
        wins[nm] = nc.dram_tensor(nm, shp, F32, kind="ExternalInput")
    eye32 = nc.dram_tensor("eye32", [128, 128], F32, kind="ExternalInput")
    psw_in = nc.dram_tensor("psw", [32, 32], F32, kind="ExternalInput")

    out_t = nc.dram_tensor("out", [S, D], F32, kind="ExternalOutput")

    x1t_h = nc.dram_tensor("x1t_h", [128, NDP * S], F32, kind="Internal")
    x2t_h = nc.dram_tensor("x2t_h", [128, NDP * S], F32, kind="Internal")
    qt_h = nc.dram_tensor("qt_h", [128, NDP * S], BF16, kind="Internal")
    kt_h = nc.dram_tensor("kt_h", [128, NDP * S], BF16, kind="Internal")
    v_h = nc.dram_tensor("v_h", [128, NDP * 32 * 130], BF16, kind="Internal")
    ao_h = nc.dram_tensor("ao_h", [128, NDP * S], BF16, kind="Internal")
    w1q_h = nc.dram_tensor("w1q_h", [128, NDP * DFF], BF16, kind="Internal")
    w2q_h = nc.dram_tensor("w2q_h", [128, NFT * D], BF16, kind="Internal")

    ctx = ExitStack()
    sems, counts = {}, {}
    for s_ in ["LD", "LD2", "LD3", "LW", "LA", "ST", "PE", "SC", "VE", "GP", "OUT"]:
        sems[s_] = ctx.enter_context(nc.semaphore(s_))
        counts[s_] = 0

    def plan():
        return Plan(nc, sems, counts)

    def sb(name, shape, dt=F32):
        return ctx.enter_context(nc.sbuf_tensor(name, shape, dt))

    def ps(name, shape, dt=F32):
        return ctx.enter_context(nc.psum_tensor(name, shape, dt))

    ones16c = sb("ones16c", [128, 1], BF16)
    ones_colf = sb("ones_colf", [128, 1], F32)
    ones32r = sb("ones32r", [1, 128], F32)
    ones65r = sb("ones65r", [65, 128], F32)
    ones16r = sb("ones16r", [1, 512], BF16)
    epst = sb("epst", [1, 1], F32)
    halfpi32 = sb("halfpi32", [32, 1], F32)
    seye32 = sb("seye32", [128, 128], F32)
    seye16 = sb("seye16", [128, 128], BF16)
    spsw = sb("spsw", [32, 32], BF16)
    thdup = sb("thdup", [1, 32], F32)
    b1T = sb("b1T", [128, 32], F32)
    b2hT = sb("b2hT", [128, 8], F32)
    bdqT = sb("bdqT", [128, 4], F32)
    bdkvT = sb("bdkvT", [128, 4], F32)
    buqT = sb("buqT", [128, 8], F32)
    bqrT = sb("bqrT", [128, 8], F32)
    bQcT = sb("bQcT", [128, 8], F32)
    bukT = sb("bukT", [128, 8], F32)
    bkrT = sb("bkrT", [128, 8], F32)
    bKcT = sb("bKcT", [128, 8], F32)
    boT = sb("boT", [128, 8], F32)
    rows3 = sb("rows3", [65, 512], F32)
    row_a = rows3[0:1, :]
    row_b = rows3[32:33, :]
    row_c = rows3[64:65, :]


    pA = ps("pA", [128, 512])
    pB = ps("pB", [128, 512])
    pG = ps("pG", [128, 1024])
    pE = ps("pE", [128, 1024])
    pF = ps("pF", [128, 1024])
    pC = pG[:, 0:512]
    pD = pG[:, 512:1024]

    stk0 = ExitStack()
    braw = stk0.enter_context(nc.sbuf_tensor("braw", [32, 9 * 128], F32))
    p = plan()
    p.emit("sync", lambda E: E.dma_start(out=seye32[:, :], in_=eye32.ap()), inc=[("LD", 16)])
    p.emit("gpsimd", lambda E: E.dma_start(out=seye16[:, :], in_=eye32.ap()), inc=[("LD", 16)])
    p.emit("gpsimd", lambda E: E.dma_start(out=spsw[:, :], in_=psw_in.ap()), inc=[("LD", 16)])
    for t, val in [(ones16c, 1.0), (ones_colf, 1.0), (ones32r, 1.0), (ones65r, 1.0),
                   (ones16r, 1.0), (epst, EPS), (halfpi32, 0.5 * PI)]:
        p.emit("vector", lambda E, t=t, val=val: E.memset(t[:, :], val), inc=["VE"])
    p.emit("gpsimd", lambda E: E.iota(row_a[0:1, 0:16], [[1, 16]], channel_multiplier=0,
                                      allow_small_or_imprecise_dtypes=True), inc=["GP"])
    p.emit("scalar", lambda E: E.activation(row_b[0:1, 0:16], row_a[0:1, 0:16], AF.Exp,
                                            scale=-2.0 * LN1E4),
           wait=[("GP", p.cnt("GP"))], inc=["SC"])
    p.emit("vector", lambda E: E.tensor_scalar_mul(row_c[0:1, 0:16], row_b[0:1, 0:16], 1.0 / 512.0),
           wait=[("SC", p.cnt("SC"))], inc=["VE"])
    ve_th = p.cnt("VE")
    p.emit("vector", lambda E: E.tensor_copy(AP(thdup, 0, [[32, 1], [2, 16]]), row_c[0:1, 0:16]),
           wait=[("VE", ve_th)], inc=["VE"])
    p.emit("vector", lambda E: E.tensor_copy(AP(thdup, 1, [[32, 1], [2, 16]]), row_c[0:1, 0:16]),
           wait=[("VE", ve_th)], inc=["VE"])

    bias_jobs = [
        ("b1", b1T, 32, None), ("b2", b2hT, 8, 0.5),
        ("bdq", bdqT, 4, None), ("bdkv", bdkvT, 4, None),
        ("buq", buqT, 8, None), ("bqr", bqrT, 8, None),
        ("buk", bukT, 8, None), ("bkr", bkrT, 8, None),
        ("bo", boT, 8, None),
    ]
    for bi, (nm, dst, nb, scl) in enumerate(bias_jobs):
        h = wins[nm]
        p.emit("sync",
               lambda E, h=h, nb=nb, bi=bi: E.dma_start(
                   out=braw[0:nb, bi * 128:(bi + 1) * 128],
                   in_=AP(h, 0, [[128, nb], [1, 128]])),
               inc=[("LD", 16)])
    ld_b = p.cnt("LD")
    ve0 = p.cnt("VE")
    bevac = {}
    for bi, (nm, dst, nb, scl) in enumerate(bias_jobs):
        pp = pA if bi % 2 == 0 else pB
        wt = [("LD", ld_b), ("VE", bevac.get(bi - 2, ve0))]
        p.emit("tensor",
               lambda E, nb=nb, bi=bi, pp=pp: E.transpose(
                   pp[:, 0:nb], braw[0:nb, bi * 128:(bi + 1) * 128],
                   seye32[0:nb, 0:nb]),
               wait=wt, inc=["PE"])
        pe_tr = p.cnt("PE")
        if scl is None:
            p.emit("vector",
                   lambda E, dst=dst, nb=nb, pp=pp: E.tensor_copy(dst[:, 0:nb], pp[:, 0:nb]),
                   wait=[("PE", pe_tr)], inc=["VE"])
        else:
            p.emit("vector",
                   lambda E, dst=dst, nb=nb, scl=scl, pp=pp: E.tensor_scalar_mul(
                       dst[:, 0:nb], pp[:, 0:nb], scl),
                   wait=[("PE", pe_tr)], inc=["VE"])
        bevac[bi] = p.cnt("VE")
    p.emit("vector", lambda E: E.tensor_add(bQcT[:, :], buqT[:, :], bqrT[:, :]),
           wait=[("VE", p.cnt("VE"))], inc=["VE"])
    p.emit("vector", lambda E: E.tensor_add(bKcT[:, :], bukT[:, :], bkrT[:, :]),
           wait=[("VE", p.cnt("VE"))], inc=["VE"])
    p.run("p0")
    stk0.close()

    w1q_sb = [None]
    w2q_sb = [None]
    wq_st_mark = [0]
    ffn1_st = {}
    ao_st_all = [0]

    def q_geom(n_row_tiles, row_cols):
        PC = 2048
        if row_cols <= PC:
            tpp, ncc = PC // row_cols, 1
        else:
            tpp, ncc = 1, row_cols // PC
        return tpp, ncc, (n_row_tiles // tpp) * ncc

    def q_piece_ap(h, row_cols, tpp, ncc, pi):
        PC = 2048
        if ncc == 1:
            return AP(h, pi * tpp * 128 * row_cols,
                      [[row_cols, 128], [128 * row_cols, tpp], [1, row_cols]])
        ti, cc = divmod(pi, ncc)
        return AP(h, ti * 128 * row_cols + cc * PC,
                  [[row_cols, 128], [128 * row_cols, 1], [1, PC]])

    def q_piece_blocks(row_cols, tpp, ncc, pi):
        PC = 2048
        out = []
        if ncc == 1:
            for tl in range(tpp):
                for sub in range(row_cols // 128):
                    out.append((pi * tpp + tl, sub, tl * row_cols + sub * 128))
        else:
            ti, cc = divmod(pi, ncc)
            for s in range(PC // 128):
                out.append((ti, cc * (PC // 128) + s, s * 128))
        return out

    def q_eng_of(eng, pi):
        if eng == "split":
            return ("sync", "LD") if pi % 2 == 0 else ("scalar", "LD3")
        return (eng, "LD" if eng == "sync" else "LD2")

    def quant_pass1(p, wname, wpc, eng, gcol, n_row_tiles, row_cols):
        h = wins[wname]
        tpp, ncc, npieces = q_geom(n_row_tiles, row_cols)
        gred = {}
        for pi in range(npieces):
            b = pi % 2
            e, ldsem = q_eng_of(eng, pi)
            w = [("VE", gred[pi - 2])] if pi >= 2 else []
            p.emit(e,
                   lambda E, pi=pi, b=b: E.dma_start(
                       out=wpc[b][:, :], in_=q_piece_ap(h, row_cols, tpp, ncc, pi)),
                   wait=w, inc=[(ldsem, 16)])
            p.emit("vector",
                   lambda E, pi=pi, b=b: E.reduce_sum(gtmp[:, gcol + pi:gcol + pi + 1],
                                                      wpc[b][:, :], axis=AXX,
                                                      apply_absolute_value=True),
                   wait=[(ldsem, p.cnt(ldsem))], inc=["VE"])
            gred[pi] = p.cnt("VE")
        return gred

    def quant_gamma(p, gcol, npieces, nelem, rcol, pcol, thrcol):
        p.emit("tensor",
               lambda E: E.matmul(pD[0:1, pcol:pcol + npieces], ones_colf[:, :],
                                  gtmp[:, gcol:gcol + npieces], start=True, stop=True),
               wait=[("VE", p.cnt("VE"))], inc=["PE"])
        p.emit("vector",
               lambda E: E.reduce_sum(row_a[0:1, rcol:rcol + 1],
                                      pD[0:1, pcol:pcol + npieces], axis=AXX),
               wait=[("PE", p.cnt("PE"))], inc=["VE"])
        p.emit("vector",
               lambda E: E.tensor_scalar(row_a[0:1, rcol + 1:rcol + 2],
                                         row_a[0:1, rcol:rcol + 1],
                                         0.5 / nelem, 0.5 * EPS, ALU.mult, ALU.add),
               wait=[("VE", p.cnt("VE"))], inc=["VE"])
        p.emit("vector",
               lambda E: E.tensor_scalar_mul(row_a[0:1, rcol + 2:rcol + 3],
                                             row_a[0:1, rcol + 1:rcol + 2], -1.0),
               wait=[("VE", p.cnt("VE"))], inc=["VE"])
        ve_thr = p.cnt("VE")
        p.emit("tensor",
               lambda E: E.matmul(pD[:, pcol + 100:pcol + 102], ones32r[0:1, :],
                                  row_a[0:1, rcol + 1:rcol + 3], start=True, stop=True),
               wait=[("VE", ve_thr)], inc=["PE"])
        p.emit("vector",
               lambda E: E.tensor_copy(gtmp[:, thrcol:thrcol + 2],
                                       pD[:, pcol + 100:pcol + 102]),
               wait=[("PE", p.cnt("PE"))], inc=["VE"])

    def quant_pass2(p, wname, wpc, eng, qb, qsb, qsb_w, qhbm, thrcol, dst_col,
                    n_row_tiles, row_cols, first_ld_w, qst):
        utmp, qnat = qb
        """Re-stream W, quantize against thresholds, transpose to qsb +
        store to qhbm.  Returns per-piece emitters for interleaving."""
        h = wins[wname]
        tpp, ncc, npieces = q_geom(n_row_tiles, row_cols)
        ve_thrb = [None]
        qdone_mark = {}
        tr_pe_mark = {}

        def emit_piece(pi):
            if ve_thrb[0] is None:
                ve_thrb[0] = p.cnt("VE")
            b = pi % 2
            e, ldsem = q_eng_of(eng, pi)
            w = ([("VE", qdone_mark[(pi - 2, 1)])] if pi >= 2 else list(first_ld_w))
            p.emit(e,
                   lambda E, pi=pi, b=b: E.dma_start(
                       out=wpc[b][:, :], in_=q_piece_ap(h, row_cols, tpp, ncc, pi)),
                   wait=w, inc=[(ldsem, 16)])
            ld = p.cnt(ldsem)
            blocks = q_piece_blocks(row_cols, tpp, ncc, pi)
            for si in range(2):
                wq = [(ldsem, ld), ("VE", ve_thrb[0]),
                      ("VE", qdone_mark.get((pi - 2, si), 0)),
                      ("VE", qdone_mark.get((pi, si - 1),
                                            qdone_mark.get((pi - 1, 1), qst[0])))]
                wq.append(("PE", tr_pe_mark.get((pi, si - 1),
                                                tr_pe_mark.get((pi - 1, 1), qst[1]))))
                p.emit("vector",
                       lambda E, b=b, si=si: E.tensor_single_scalar(
                           utmp[:, :], wpc[b][:, si * 1024:(si + 1) * 1024],
                           gtmp[:, thrcol + 1:thrcol + 2], ALU.is_lt),
                       wait=wq, inc=["VE"])
                p.emit("vector",
                       lambda E, b=b, si=si: E.scalar_tensor_tensor(
                           qnat[:, :], wpc[b][:, si * 1024:(si + 1) * 1024],
                           gtmp[:, thrcol:thrcol + 1], utmp[:, :],
                           ALU.is_gt, ALU.subtract),
                       wait=[("VE", p.cnt("VE"))], inc=["VE"])
                qdone_mark[(pi, si)] = p.cnt("VE")
                ve_q = p.cnt("VE")
                sblocks = [blk for blk in blocks
                           if si * 1024 <= blk[2] < (si + 1) * 1024]
                for g0 in range(0, len(sblocks), 4):
                    grp = sblocks[g0:g0 + 4]
                    reg = trreg[0]
                    trreg[0] = (trreg[0] + 1) % 4
                    pt = pT16ab[reg]
                    pe_w = [("VE", ve_q), ("SC", trgrp_mark[reg])]
                    for gi, (ti, sub, qcol) in enumerate(grp):
                        p.emit("tensor",
                               lambda E, qcol=qcol, gi=gi, si=si, pt=pt:
                               E.transpose(pt[:, gi * 128:(gi + 1) * 128],
                                           qnat[:, qcol - si * 1024:qcol - si * 1024 + 128],
                                           seye16[:, :]),
                               wait=pe_w if gi == 0 else [], inc=["PE"])
                    pe_t = p.cnt("PE")
                    d0 = dst_col(grp[0][0], grp[0][1])
                    d1 = dst_col(grp[1][0], grp[1][1])
                    p.emit("scalar",
                           lambda E, d0=d0, st=d1 - d0, pt=pt: E.copy(
                               AP(qsb, d0, [[qsb_w, 128], [st, 4], [1, 128]]),
                               pt[:, 0:512]),
                           wait=[("PE", pe_t)], inc=["SC"])
                    trgrp_mark[reg] = p.cnt("SC")
                tr_pe_mark[(pi, si)] = p.cnt("PE")

        def emit_store():
            p.emit("sync", lambda E: E.dma_start(out=qhbm.ap(), in_=qsb[:, :]),
                   wait=[("SC", p.cnt("SC"))], inc=[("ST", 16)])

        return emit_piece, emit_store, npieces

    stk1 = ExitStack()
    _w1q = stk1.enter_context(nc.sbuf_tensor("w1q", [128, NDP * DFF], BF16))
    _w2q = stk1.enter_context(nc.sbuf_tensor("w2q", [128, NFT * D], BF16))
    stk1b = ExitStack()
    gtmp = stk1b.enter_context(nc.sbuf_tensor("gtmp", [128, 40], F32))
    wp0 = stk1b.enter_context(nc.sbuf_tensor("wp0", [128, 2048], F32))
    wp1 = stk1b.enter_context(nc.sbuf_tensor("wp1", [128, 2048], F32))
    wp2 = stk1b.enter_context(nc.sbuf_tensor("wp2", [128, 2048], F32))
    wp3 = stk1b.enter_context(nc.sbuf_tensor("wp3", [128, 2048], F32))
    utmp = stk1b.enter_context(nc.sbuf_tensor("utmp", [128, 1024], F32))
    qnat = stk1b.enter_context(nc.sbuf_tensor("qnat", [128, 1024], BF16))
    utmp2 = stk1b.enter_context(nc.sbuf_tensor("utmp2", [128, 1024], F32))
    qnat2 = stk1b.enter_context(nc.sbuf_tensor("qnat2", [128, 1024], BF16))
    pT16ab = [pC[:, 0:256].bitcast(BF16), pD[:, 0:256].bitcast(BF16),
              pC[:, 256:512].bitcast(BF16), pD[:, 256:512].bitcast(BF16)]
    if True:
        w1q_sb[0] = _w1q
        w2q_sb[0] = _w2q
        trgrp_mark = [0, 0, 0, 0]
        trreg = [0]
        qst = [0, 0]
        p = plan()
        g2 = quant_pass1(p, "w2", [wp2, wp3], "gpsimd", 16, NDP, DFF)
        g1 = quant_pass1(p, "w1", [wp0, wp1], "split", 0, NFT, D)
        quant_gamma(p, 0, 16, NFT * 128 * D, 0, 0, 36)
        quant_gamma(p, 16, 16, NDP * 128 * DFF, 4, 16, 38)
        em1, st1, np1 = quant_pass2(p, "w1", [wp0, wp1], "split", (utmp, qnat),
                                    _w1q, NDP * DFF, w1q_h, 36,
                                    lambda ti, sub: sub * DFF + ti * 128, NFT, D,
                                    [("VE", g1[14]), ("VE", g1[15])], qst)
        em2, st2, np2 = quant_pass2(p, "w2", [wp2, wp3], "gpsimd", (utmp2, qnat2),
                                    _w2q, NFT * D, w2q_h, 38,
                                    lambda ti, sub: sub * D + ti * 128, NDP, DFF,
                                    [("VE", g2[14]), ("VE", g2[15])], qst)
        for pi in range(max(np1, np2)):
            if pi < np1:
                em1(pi)
            if pi < np2:
                em2(pi)
        st1()
        st2()
        wq_st_mark[0] = p.cnt("ST")
        p.run("p1")
        stk1b.close()


        def ffn_phase(p, src_kind, dst_kind, bufs, wq_h=None, wq_d=None,
                      first_ve=0, first_sc=0):
            HSC = SCH // 2
            xnat = bufs.get("xnat")
            xtb = bufs["xtb"]
            xb, xsq, yb, ht = bufs["xb"], bufs["xsq"], bufs["yb"], bufs["ht"]
            tmp512, etmp, onat = bufs["tmp512"], bufs["etmp"], bufs.get("onat")
            st_marks = {}
            trev_last = [first_sc, first_sc]
            xb_pe_free = [0, 0]
            xsq_pe_free = [0, 0]
            xtb_free = {}
            onat_free = [("OUT", 0), ("OUT", 0)]
            onat_ev = [0]
            xnat_free = {}
            xnld = {}
            ld_x2 = {}
            xt_wait_map = {}
            ve_y_map = {}
            ve_res_map = {}
            chain_ve = {}
            trev_map = {}
            pF_free = [first_ve]
            pEs_free = [0]
            exp_pe = [0]
            sc_dacc_done = [first_sc]
            sc_h_map = {}
            dacc_sc = {}
            dacc_ve = {}
            relu_marks = {}
            h_bank_free = [0, 0]
            o_bank_free = [0, 0]
            eb_free = [0, 0]

            def emit_xnload(hkey):
                c2, hf2 = divmod(hkey, 2)
                w = [("PE", xnat_free[hkey - 1])] if hkey >= 1 else []
                p.emit("sync",
                       lambda E, c2=c2, hf2=hf2: E.dma_start(
                           out=xnat[0][:, :],
                           in_=AP(x_in, (c2 * SCH + hf2 * 256) * D,
                                  [[D, 128], [128 * D, 2], [1, D]])),
                       wait=w, inc=[("LD", 16)])
                xnld[hkey] = p.cnt("LD")

            st_at_entry = p.cnt("ST")

            def emit_x2load(c):
                cb = c % 2
                w = [xtb_free[c - 2]] if c >= 2 else [("ST", st_at_entry)]
                p.emit("sync",
                       lambda E, c=c, cb=cb: E.dma_start(
                           out=xtb[cb][:, :],
                           in_=AP(x2t_h, c * SCH, [[NDP * S, 128], [S, NDP], [1, SCH]])),
                       wait=w, inc=[("LD", 16)])
                ld_x2[c] = p.cnt("LD")

            def emit_trans(c):
                cb = c % 2
                for hf in range(2):
                    hkey = 2 * c + hf
                    ld_h = xnld[hkey]
                    for dp in range(NDP):
                        pp = pC if dp % 2 == 0 else pD
                        wt = [("LD", ld_h), ("SC", sc_dacc_done[0]),
                              ("SC", trev_last[dp % 2]), ("VE", first_ve)]
                        for st_ in range(2):
                            p.emit("tensor",
                                   lambda E, st_=st_, dp=dp, pp=pp, cb=cb:
                                   E.transpose(pp[:, st_ * 128:(st_ + 1) * 128],
                                               xnat[0][:, st_ * D + dp * 128:
                                                       st_ * D + dp * 128 + 128],
                                               seye32[:, :]),
                                   wait=wt if st_ == 0 else [], inc=["PE"])
                        pe_t = p.cnt("PE")
                        wev = [("PE", pe_t)]
                        if dp == 0 and hf == 0 and c >= 2:
                            wev.append(xtb_free[c - 2])
                        p.emit("scalar",
                               lambda E, dp=dp, hf=hf, pp=pp, cb=cb:
                               E.copy(xtb[cb][:, dp * SCH + hf * 256:
                                              dp * SCH + hf * 256 + 256],
                                      pp[:, 0:256]),
                               wait=wev, inc=["SC"])
                        trev_last[dp % 2] = p.cnt("SC")
                    xnat_free[hkey] = p.cnt("PE")
                    if hkey + 1 < 2 * NCH:
                        emit_xnload(hkey + 1)
                xt_wait_map[c] = [("SC", p.cnt("SC"))]
                trev_map[c] = p.cnt("SC")

            def emit_stats(c):
                cb = c % 2
                xt_wait = (xt_wait_map[c] if src_kind == "nat"
                           else [("LD", ld_x2[c] if c >= 2 else ld_x2[1])])
                for dp in range(NDP):
                    sbb = xb[dp % 2]
                    wcp = xt_wait + [("PE", xb_pe_free[dp % 2])]
                    p.emit("vector",
                           lambda E, cb=cb, dp=dp, sbb=sbb:
                           E.tensor_copy(sbb[:, :], xtb[cb][:, dp * SCH:(dp + 1) * SCH]),
                           wait=wcp, inc=["VE"])
                    wmm = [("VE", p.cnt("VE"))]
                    if dp == 0:
                        wmm.append(("VE", pEs_free[0]))
                    p.emit("tensor",
                           lambda E, dp=dp, sbb=sbb: E.matmul(pE[0:1, 0:512], ones16c[:, :],
                                                              sbb[:, :],
                                                              start=(dp == 0),
                                                              stop=(dp == NDP - 1)),
                           wait=wmm, inc=["PE"])
                    xb_pe_free[dp % 2] = p.cnt("PE")
                for dp in range(NDP):
                    sqq = xsq[dp % 2]
                    wcp = xt_wait + [("PE", xsq_pe_free[dp % 2])]
                    p.emit("scalar",
                           lambda E, cb=cb, dp=dp, sqq=sqq:
                           E.activation(sqq[:, :], xtb[cb][:, dp * SCH:(dp + 1) * SCH],
                                        AF.Square),
                           wait=wcp, inc=["SC"])
                    p.emit("tensor",
                           lambda E, dp=dp, sqq=sqq: E.matmul(pE[0:1, 512:1024], ones16c[:, :],
                                                              sqq[:, :],
                                                              start=(dp == 0),
                                                              stop=(dp == NDP - 1)),
                           wait=[("SC", p.cnt("SC"))], inc=["PE"])
                    xsq_pe_free[dp % 2] = p.cnt("PE")

            def emit_chain(c):
                pe_stats = p.cnt("PE")
                p.emit("vector",
                       lambda E: E.tensor_scalar_mul(row_a[0:1, 0:512], pE[0:1, 0:512], 1.0 / D),
                       wait=[("PE", pe_stats), ("PE", exp_pe[0])], inc=["VE"])
                p.emit("vector",
                       lambda E: E.tensor_mul(row_b[0:1, 0:512], row_a[0:1, 0:512],
                                              row_a[0:1, 0:512]),
                       wait=[("VE", p.cnt("VE"))], inc=["VE"])
                p.emit("vector",
                       lambda E: E.scalar_tensor_tensor(row_c[0:1, 0:512], pE[0:1, 512:1024],
                                                        1.0 / D, row_b[0:1, 0:512],
                                                        ALU.mult, ALU.subtract),
                       wait=[("VE", p.cnt("VE"))], inc=["VE"])
                pEs_free[0] = p.cnt("VE")
                p.emit("scalar",
                       lambda E: E.activation(row_b[0:1, 0:512], row_c[0:1, 0:512], AF.Sqrt,
                                              bias=epst[:, :]),
                       wait=[("VE", p.cnt("VE"))], inc=["SC"])
                p.emit("vector", lambda E: E.reciprocal(row_c[0:1, 0:512], row_b[0:1, 0:512]),
                       wait=[("SC", p.cnt("SC"))], inc=["VE"])
                chain_ve[c] = p.cnt("VE")

            def emit_exp_y(c):
                cb = c % 2
                p.emit("tensor",
                       lambda E: E.matmul(pF[:, 0:512], ones32r[0:1, :], row_a[0:1, 0:512],
                                          start=True, stop=True),
                       wait=[("VE", chain_ve[c]), ("VE", pF_free[0])], inc=["PE"])
                p.emit("tensor",
                       lambda E: E.matmul(pF[:, 512:1024], ones65r[64:65, :], row_c[0:1, 0:512],
                                          start=True, stop=True),
                       wait=[("VE", chain_ve[c])], inc=["PE"])
                pe_exp = p.cnt("PE")
                exp_pe[0] = pe_exp
                for dp in range(NDP):
                    p.emit("vector",
                           lambda E, dp=dp, cb=cb: E.tensor_sub(tmp512[:, :],
                                                                xtb[cb][:, dp * SCH:(dp + 1) * SCH],
                                                                pF[:, 0:512]),
                           wait=[("PE", pe_exp)] if dp == 0 else [], inc=["VE"])
                    p.emit("vector",
                           lambda E, dp=dp: E.tensor_mul(yb[:, dp * SCH:(dp + 1) * SCH],
                                                         tmp512[:, :], pF[:, 512:1024]),
                           inc=["VE"])
                pF_free[0] = p.cnt("VE")
                ve_y_map[c] = p.cnt("VE")

            def emit_h(c, half):
                ve_y = ve_y_map[c]
                for ft in range(NFT):
                    pp = pA if ft % 2 == 0 else pB
                    wt = [("VE", ve_y), ("SC", h_bank_free[ft % 2]),
                          ("VE", onat_ev[0])]
                    if c == 0 and wq_h is not None:
                        wt.append(wq_h[ft])
                    for dp in range(NDP):
                        p.emit("tensor",
                               lambda E, ft=ft, dp=dp, pp=pp, half=half:
                               E.matmul(pp[:, 0:HSC],
                                        w1q_sb[0][:, dp * DFF + ft * 128:dp * DFF + ft * 128 + 128],
                                        yb[:, dp * SCH + half * HSC:dp * SCH + half * HSC + HSC],
                                        start=(dp == 0), stop=(dp == NDP - 1)),
                               wait=wt if dp == 0 else [], inc=["PE"])
                    pe_h = p.cnt("PE")
                    p.emit("scalar",
                           lambda E, ft=ft, pp=pp: E.activation(ht[:, ft * HSC:(ft + 1) * HSC],
                                                                pp[:, 0:HSC], AF.Relu,
                                                                bias=b1T[:, ft:ft + 1]),
                           wait=[("PE", pe_h)], inc=["SC"])
                    h_bank_free[ft % 2] = p.cnt("SC")
                sc_h_map[(c, half)] = p.cnt("SC")

            def emit_out(c, half):
                cb = c % 2
                sc_h = sc_h_map[(c, half)]
                for dp in range(NDP):
                    pp = pC if dp % 2 == 0 else pD
                    wt = [("SC", sc_h), ("SC", o_bank_free[dp % 2])]
                    if c == 0 and wq_d is not None:
                        wt.append(wq_d[dp])
                    if src_kind == "nat" and dp < 2 and c + 1 in trev_map:
                        wt.append(("SC", trev_map[c + 1]))
                    for fp in range(NFT):
                        p.emit("tensor",
                               lambda E, dp=dp, fp=fp, pp=pp:
                               E.matmul(pp[:, 0:HSC],
                                        w2q_sb[0][:, fp * D + dp * 128:fp * D + dp * 128 + 128],
                                        ht[:, fp * HSC:(fp + 1) * HSC],
                                        start=(fp == 0), stop=(fp == NFT - 1)),
                               wait=wt if fp == 0 else [], inc=["PE"])
                    pe_d = p.cnt("PE")
                    eb = etmp[dp % 2]
                    wev = [("PE", pe_d), ("VE", eb_free[dp % 2])]
                    p.emit("scalar",
                           lambda E, dp=dp, pp=pp, eb=eb:
                           E.activation(eb[:, 0:HSC], pp[:, 0:HSC], AF.Identity, scale=0.5,
                                        bias=b2hT[:, dp:dp + 1]),
                           wait=wev, inc=["SC"])
                    o_bank_free[dp % 2] = p.cnt("SC")
                    sc_e = p.cnt("SC")
                    p.emit("vector",
                           lambda E, dp=dp, cb=cb, eb=eb, half=half:
                           E.tensor_add(xtb[cb][:, dp * SCH + half * HSC:
                                                dp * SCH + half * HSC + HSC],
                                        xtb[cb][:, dp * SCH + half * HSC:
                                                dp * SCH + half * HSC + HSC],
                                        eb[:, 0:HSC]),
                           wait=[("SC", sc_e)], inc=["VE"])
                    eb_free[dp % 2] = p.cnt("VE")
                if half == 1:
                    sc_dacc_done[0] = p.cnt("SC")
                    ve_res_map[c] = p.cnt("VE")

            def emit_store(c):
                cb = c % 2
                p.emit("scalar",
                       lambda E, c=c, cb=cb: E.dma_start(
                           out=AP(x1t_h, c * SCH, [[NDP * S, 128], [S, NDP], [1, SCH]]),
                           in_=xtb[cb][:, :]),
                       wait=[("VE", ve_res_map[c])], inc=[("ST", 16)])
                st_marks[c] = p.cnt("ST")
                xtb_free[c] = ("ST", p.cnt("ST"))
                if "st_out" in bufs:
                    bufs["st_out"][c] = p.cnt("ST")

            def emit_outstore(c):
                cb = c % 2
                for sg in range(4):
                    ob = sg % 2
                    for half in range(2):
                        php = pA if half == 0 else pB
                        wt = [("VE", ve_res_map[c]), ("VE", onat_ev[0]),
                              ("SC", h_bank_free[half])]
                        wt.append(onat_free[ob])
                        for j in range(4):
                            dp = half * 4 + j
                            p.emit("tensor",
                                   lambda E, sg=sg, dp=dp, j=j, php=php, cb=cb:
                                   E.transpose(php[:, j * 128:(j + 1) * 128],
                                               xtb[cb][:, dp * SCH + sg * 128:
                                                       dp * SCH + sg * 128 + 128],
                                               seye32[:, :]),
                                   wait=wt if j == 0 else [], inc=["PE"])
                        pe_t = p.cnt("PE")
                        p.emit("vector",
                               lambda E, ob=ob, half=half, php=php:
                               E.tensor_copy(onat[:, ob * D + half * 512:
                                                  ob * D + half * 512 + 512],
                                             php[:, :]),
                               wait=[("PE", pe_t)], inc=["VE"])
                        onat_ev[0] = p.cnt("VE")
                    p.emit("sync",
                           lambda E, c=c, sg=sg, ob=ob: E.dma_start(
                               out=AP(out_t, (c * SCH + sg * 128) * D,
                                      [[D, 128], [1, D]]),
                               in_=onat[:, ob * D:(ob + 1) * D]),
                           wait=[("VE", onat_ev[0])], inc=[("OUT", 16)])
                    onat_free[ob] = ("OUT", p.cnt("OUT"))
                xtb_free[c] = ("PE", p.cnt("PE"))
                st_marks[c] = p.cnt("ST")

            if src_kind == "nat":
                emit_xnload(0)
                emit_trans(0)
            else:
                emit_x2load(0)
                emit_x2load(1)
                pass
            emit_stats(0)
            emit_chain(0)
            emit_exp_y(0)
            pend_outstore = [None]
            PIPE = True
            if PIPE:
                for c in range(NCH):
                    emit_h(c, 0)
                    if c + 1 < NCH:
                        if src_kind == "nat":
                            emit_trans(c + 1)
                        emit_stats(c + 1)
                        emit_chain(c + 1)
                    emit_out(c, 0)
                    emit_h(c, 1)
                    if c + 1 < NCH:
                        emit_exp_y(c + 1)
                    emit_out(c, 1)
                    if dst_kind == "x1t":
                        emit_store(c)
                    else:
                        emit_outstore(c)
                        if 2 <= c + 2 < NCH:
                            emit_x2load(c + 2)
            else:
                for c in range(NCH):
                    emit_h(c, 0)
                    emit_out(c, 0)
                    emit_h(c, 1)
                    emit_out(c, 1)
                    if dst_kind == "x1t":
                        emit_store(c)
                    else:
                        emit_outstore(c)
                    if c + 1 < NCH:
                        if src_kind == "nat":
                            emit_trans(c + 1)
                        emit_stats(c + 1)
                        emit_chain(c + 1)
                        emit_exp_y(c + 1)
                        if bufs.get("dbg_x2chk") and c + 1 == 1:
                            pass
                    if src_kind != "nat" and 2 <= c + 2 < NCH:
                        emit_x2load(c + 2)
            p.emit("gpsimd", lambda E: E.nop(),
                   wait=[("ST", p.cnt("ST")), ("OUT", p.cnt("OUT"))])

        stk2 = ExitStack()
        xnat0 = stk2.enter_context(nc.sbuf_tensor("xnat0", [128, 2 * D], F32))
        f_xt0 = stk2.enter_context(nc.sbuf_tensor("f_xt0", [128, NDP * SCH], F32))
        f_xt1 = stk2.enter_context(nc.sbuf_tensor("f_xt1", [128, NDP * SCH], F32))
        f_xb0 = stk2.enter_context(nc.sbuf_tensor("f_xb0", [128, SCH], BF16))
        f_xb1 = stk2.enter_context(nc.sbuf_tensor("f_xb1", [128, SCH], BF16))
        f_xq0 = stk2.enter_context(nc.sbuf_tensor("f_xq0", [128, SCH], BF16))
        f_xq1 = stk2.enter_context(nc.sbuf_tensor("f_xq1", [128, SCH], BF16))
        f_yb = stk2.enter_context(nc.sbuf_tensor("f_yb", [128, NDP * SCH], BF16))
        f_ht = stk2.enter_context(nc.sbuf_tensor("f_ht", [128, NFT * (SCH // 2)], BF16))
        f_et0 = stk2.enter_context(nc.sbuf_tensor("f_et0", [128, 512], F32))
        f_et1 = stk2.enter_context(nc.sbuf_tensor("f_et1", [128, 512], F32))
        f_tmp = stk2.enter_context(nc.sbuf_tensor("f_tmp", [128, 512], F32))
        if True:
            p = plan()
            trev_marks = {}
            dacc_sc = {}
            bufs = dict(xnat=[xnat0], xtb=[f_xt0, f_xt1], xb=[f_xb0, f_xb1],
                        xsq=[f_xq0, f_xq1], yb=f_yb, ht=f_ht, tmp512=f_tmp,
                        etmp=[f_et0, f_et1], st_out=ffn1_st)
            sc_q = p.cnt("SC")
            ffn_phase(p, "nat", "x1t", bufs,
                      [("SC", sc_q)] * NFT, [("SC", sc_q)] * NDP,
                      first_ve=p.cnt("VE"), first_sc=sc_q)
            p.run("ffn1")
            stk2.close()
        stk1.close()

    if debug_stage == "x1":
        return _finish_debug(nc, plan, counts, out_t, x1t_h, F32, NDP * S)
    if debug_stage in ("w1q", "w2q"):
        return _finish_debug(nc, plan, counts, out_t,
                             w1q_h if debug_stage == "w1q" else w2q_h, BF16,
                             NDP * DFF if debug_stage == "w1q" else NFT * D)

    stk3 = ExitStack()
    wdq16 = stk3.enter_context(nc.sbuf_tensor("wdq16", [128, NDP * DQ], BF16))
    wdkv16 = stk3.enter_context(nc.sbuf_tensor("wdkv16", [128, NDP * DKV], BF16))
    wuq16 = stk3.enter_context(nc.sbuf_tensor("wuq16", [128, NQP * D], BF16))
    wqr16 = stk3.enter_context(nc.sbuf_tensor("wqr16", [128, NQP * D], BF16))
    wuk16 = stk3.enter_context(nc.sbuf_tensor("wuk16", [128, NKP * D], BF16))
    wkr16 = stk3.enter_context(nc.sbuf_tensor("wkr16", [128, NDP * D], BF16))
    wuv16 = stk3.enter_context(nc.sbuf_tensor("wuv16", [128, NKP * D], BF16))
    q_x0 = stk3.enter_context(nc.sbuf_tensor("q_x0", [128, NDP * SCH], F32))
    q_x1 = stk3.enter_context(nc.sbuf_tensor("q_x1", [128, NDP * SCH], F32))
    q_xb = stk3.enter_context(nc.sbuf_tensor("q_xb", [128, NDP * SCH], BF16))
    q_xsq = stk3.enter_context(nc.sbuf_tensor("q_xsq", [128, NDP * SCH], BF16))
    q_y10 = stk3.enter_context(nc.sbuf_tensor("q_y10", [128, NDP * SCH], BF16))
    q_y11 = stk3.enter_context(nc.sbuf_tensor("q_y11", [128, NDP * SCH], BF16))
    q_cq = stk3.enter_context(nc.sbuf_tensor("q_cq", [128, NQP * SCH], BF16))
    q_ckv = stk3.enter_context(nc.sbuf_tensor("q_ckv", [128, NKP * SCH], BF16))
    q_rs = stk3.enter_context(nc.sbuf_tensor("q_rs", [128, SCH], BF16))
    k_rs = stk3.enter_context(nc.sbuf_tensor("k_rs", [128, SCH], BF16))
    qsp0 = stk3.enter_context(nc.sbuf_tensor("qsp0", [128, SCH], BF16))
    qsp1 = stk3.enter_context(nc.sbuf_tensor("qsp1", [128, SCH], BF16))
    ksp0 = stk3.enter_context(nc.sbuf_tensor("ksp0", [128, SCH], BF16))
    ksp1 = stk3.enter_context(nc.sbuf_tensor("ksp1", [128, SCH], BF16))
    vga = stk3.enter_context(nc.sbuf_tensor("vga", [128, NDP * 520], BF16))
    vgb = stk3.enter_context(nc.sbuf_tensor("vgb", [128, NDP * 520], BF16))
    bvb = stk3.enter_context(nc.sbuf_tensor("bvb", [128, D], F32))
    q_tmp = stk3.enter_context(nc.sbuf_tensor("q_tmp", [128, 512], F32))
    Ctab = stk3.enter_context(nc.sbuf_tensor("Ctab", [32, SCH], BF16))
    Stab = stk3.enter_context(nc.sbuf_tensor("Stab", [32, SCH], BF16))
    msb = stk3.enter_context(nc.sbuf_tensor("msb", [32, SCH], F32))
    rrsb = stk3.enter_context(nc.sbuf_tensor("rrsb", [32, SCH], F32))
    fx1 = stk3.enter_context(nc.sbuf_tensor("fx1", [32, SCH], F32))
    fx2 = stk3.enter_context(nc.sbuf_tensor("fx2", [32, SCH], F32))
    fx3 = stk3.enter_context(nc.sbuf_tensor("fx3", [32, SCH], F32))
    swsb = stk3.enter_context(nc.sbuf_tensor("swsb", [32, SCH], BF16))
    posrow = stk3.enter_context(nc.sbuf_tensor("posrow", [1, SCH], F32))
    bvr32 = stk3.enter_context(nc.sbuf_tensor("bvr32", [1, D], F32))
    if True:
        p = plan()
        for dst, nm, rows, cols, nt in [
            (wdq16, "wdq", D, DQ, NDP), (wdkv16, "wdkv", D, DKV, NDP),
            (wuq16, "wuq", DQ, D, NQP), (wqr16, "wqr", DQ, D, NQP),
            (wuk16, "wuk", DKV, D, NKP), (wkr16, "wkr", D, D, NDP),
            (wuv16, "wuv", DKV, D, NKP),
        ]:
            p.emit("gpsimd",
                   lambda E, dst=dst, nm=nm, cols=cols, nt=nt:
                   E.dma_start(out=dst[:, :],
                               in_=AP(wins[nm], 0, [[cols, 128], [128 * cols, nt], [1, cols]])),
                   inc=[("LD", 16)])
        p.emit("sync", lambda E: E.dma_start(out=bvr32[0:1, :],
                                             in_=wins["buv"].ap().unsqueeze(0)),
               inc=[("LD", 16)])
        ld_w = p.cnt("LD")
        for vst in (vga, vgb):
            for off in (64, 129):
                p.emit("vector",
                       lambda E, vst=vst, off=off: E.memset(
                           AP(vst, off, [[NDP * 520, 128], [520, 8], [130, 4], [1, 1]]), 1.0),
                       inc=["VE"])
        p.emit("tensor",
               lambda E: E.matmul(pF[:, 0:512], ones32r[0:1, :],
                                  bvr32[0:1, 0:512], start=True, stop=True),
               wait=[("LD", ld_w), ("VE", p.cnt("VE")), ("SC", p.cnt("SC"))],
               inc=["PE"])
        p.emit("tensor",
               lambda E: E.matmul(pF[:, 512:1024], ones32r[0:1, :],
                                  bvr32[0:1, 512:1024], start=True, stop=True),
               inc=["PE"])
        pe_bvb = p.cnt("PE")
        p.emit("vector", lambda E: E.tensor_copy(bvb[:, 0:512], pF[:, 0:512]),
               wait=[("PE", pe_bvb)], inc=["VE"])
        p.emit("vector", lambda E: E.tensor_copy(bvb[:, 512:1024], pF[:, 512:1024]),
               inc=["VE"])
        gp_bvb = 0
        bvb_ve = p.cnt("VE")

        x1c = [q_x0, q_x1]
        x1c_free = {}
        pe_stats_prev = [0]
        pEs_free = [0]
        pF_free = [bvb_ve]
        tabfree_ve = [0]
        ropefree_sc = [0]
        qsp_st = {}
        ksp_st = {}
        vst_st = {}
        v_evac_last = [0]
        ct_ev_prev = [0, 0]
        pe_posrow_used = [0]
        qr_sc_free = [0]
        kr_sc_free = [0]
        combine_ve = [0, 0]
        ld_x1 = {}

        def emit_x1load(c):
            cb = c % 2
            w = [("ST", ffn1_st[c])]
            if c >= 2:
                w.append(("VE", x1c_free[c - 2]))
            p.emit("sync",
                   lambda E, c=c, cb=cb: E.dma_start(
                       out=x1c[cb][:, :],
                       in_=AP(x1t_h, c * SCH, [[NDP * S, 128], [S, NDP], [1, SCH]])),
                   wait=w, inc=[("LD", 16)])
            ld_x1[c] = p.cnt("LD")

        emit_x1load(0)
        emit_x1load(1)
        q_y1b = [q_y10, q_y11]
        ve_y_q = {}
        sc_ct_q = {}
        sc_tab_q = {}
        ln_chain_ve = {}

        def emit_tables(c):
            p.emit("gpsimd",
                   lambda E, c=c: E.iota(posrow[0:1, :], [[1, SCH]], base=c * SCH,
                                         channel_multiplier=0,
                                         allow_small_or_imprecise_dtypes=True),
                   wait=[("PE", pe_posrow_used[0])], inc=["GP"])
            p.emit("tensor",
                   lambda E, c=c: E.matmul(pE[0:32, 0:512], thdup[0:1, :],
                                           posrow[0:1, :],
                                           start=True, stop=True),
                   wait=[("VE", tabfree_ve[0]), ("SC", ropefree_sc[0]),
                         ("GP", p.cnt("GP"))], inc=["PE"])
            pe_posrow_used[0] = p.cnt("PE")
            pe_r = p.cnt("PE")
            p.emit("vector", lambda E: E.tensor_scalar(msb[:, :], pE[0:32, 0:512], PI, -2.0 * PI,
                                                       ALU.is_ge, ALU.mult),
                   wait=[("PE", pe_r)], inc=["VE"])
            p.emit("vector", lambda E: E.tensor_add(rrsb[:, :], msb[:, :], pE[0:32, 0:512]),
                   wait=[("VE", p.cnt("VE"))], inc=["VE"])
            p.emit("scalar", lambda E: E.activation(Stab[:, :], rrsb[:, :], AF.Sin),
                   wait=[("VE", p.cnt("VE"))], inc=["SC"])
            sc_s = p.cnt("SC")
            p.emit("vector", lambda E: E.tensor_scalar(msb[:, :], pE[0:32, 0:512], 0.5 * PI,
                                                       -2.0 * PI, ALU.is_ge, ALU.mult),
                   wait=[("SC", sc_s)], inc=["VE"])
            p.emit("vector", lambda E: E.tensor_add(rrsb[:, :], msb[:, :], pE[0:32, 0:512]),
                   wait=[("VE", p.cnt("VE"))], inc=["VE"])
            tabfree_ve[0] = p.cnt("VE")
            tab_ve = p.cnt("VE")
            p.emit("scalar", lambda E: E.activation(Ctab[:, :], rrsb[:, :], AF.Sin,
                                                    bias=halfpi32[:, :]),
                   wait=[("VE", tab_ve)], inc=["SC"])
            sc_tab_q[c] = p.cnt("SC")

        ln_cp = {}

        def emit_ln_copies(c):
            cb = c % 2
            xt_wait = [("LD", ld_x1[c] if c >= 2 else ld_x1[1])]
            p.emit("vector", lambda E, cb=cb: E.tensor_copy(q_xb[:, :], x1c[cb][:, :]),
                   wait=xt_wait + [("PE", pe_stats_prev[0])], inc=["VE"])
            p.emit("scalar", lambda E, cb=cb: E.activation(q_xsq[:, :], x1c[cb][:, :], AF.Square),
                   wait=xt_wait + [("PE", pe_stats_prev[0])], inc=["SC"])
            ln_cp[c] = (p.cnt("VE"), p.cnt("SC"))

        def emit_ln(c):
            cb = c % 2
            y1 = q_y1b[cb]
            ve_xb, sc_sq = ln_cp[c]
            for dp in range(NDP):
                p.emit("tensor",
                       lambda E, dp=dp: E.matmul(pE[0:1, 512:1024], ones16c[:, :],
                                                 q_xb[:, dp * SCH:(dp + 1) * SCH],
                                                 start=(dp == 0), stop=(dp == NDP - 1)),
                       wait=[("VE", ve_xb), ("VE", pEs_free[0])] if dp == 0 else [], inc=["PE"])
            for dp in range(NDP):
                p.emit("tensor",
                       lambda E, dp=dp: E.matmul(pF[0:1, 0:512], ones16c[:, :],
                                                 q_xsq[:, dp * SCH:(dp + 1) * SCH],
                                                 start=(dp == 0), stop=(dp == NDP - 1)),
                       wait=([("SC", sc_sq)] if dp == 0 else []) +
                            ([("VE", pF_free[0])] if dp == 0 else []),
                       inc=["PE"])
            pe_stats = p.cnt("PE")
            pe_stats_prev[0] = pe_stats
            p.emit("vector",
                   lambda E: E.tensor_scalar_mul(row_a[0:1, 0:512], pE[0:1, 512:1024], 1.0 / D),
                   wait=[("PE", pe_stats)], inc=["VE"])
            pEs_free[0] = p.cnt("VE")
            p.emit("vector", lambda E: E.tensor_mul(row_b[0:1, 0:512], row_a[0:1, 0:512],
                                                    row_a[0:1, 0:512]),
                   wait=[("VE", p.cnt("VE"))], inc=["VE"])
            p.emit("vector",
                   lambda E: E.scalar_tensor_tensor(row_c[0:1, 0:512], pF[0:1, 0:512], 1.0 / D,
                                                    row_b[0:1, 0:512], ALU.mult, ALU.subtract),
                   wait=[("VE", p.cnt("VE"))], inc=["VE"])
            p.emit("scalar", lambda E: E.activation(row_b[0:1, 0:512], row_c[0:1, 0:512], AF.Sqrt,
                                                    bias=epst[:, :]),
                   wait=[("VE", p.cnt("VE"))], inc=["SC"])
            p.emit("vector", lambda E: E.reciprocal(row_c[0:1, 0:512], row_b[0:1, 0:512]),
                   wait=[("SC", p.cnt("SC"))], inc=["VE"])
            ln_chain_ve[c] = p.cnt("VE")

        def emit_ln_y(c):
            cb = c % 2
            y1 = q_y1b[cb]
            ve_st = ln_chain_ve[c]
            p.emit("tensor", lambda E: E.matmul(pF[:, 0:512], ones32r[0:1, :], row_a[0:1, 0:512],
                                                start=True, stop=True),
                   wait=[("VE", ve_st)], inc=["PE"])
            p.emit("tensor", lambda E: E.matmul(pF[:, 512:1024], ones65r[64:65, :], row_c[0:1, 0:512],
                                                start=True, stop=True),
                   wait=[("VE", ve_st)], inc=["PE"])
            pe_exp = p.cnt("PE")
            for dp in range(NDP):
                p.emit("vector",
                       lambda E, dp=dp, cb=cb: E.tensor_sub(q_tmp[:, :],
                                                            x1c[cb][:, dp * SCH:(dp + 1) * SCH],
                                                            pF[:, 0:512]),
                       wait=[("PE", pe_exp)] if dp == 0 else [], inc=["VE"])
                p.emit("vector",
                       lambda E, dp=dp, y1=y1: E.tensor_mul(y1[:, dp * SCH:(dp + 1) * SCH],
                                                            q_tmp[:, :], pF[:, 512:1024]),
                       inc=["VE"])
            pF_free[0] = p.cnt("VE")
            x1c_free[c] = p.cnt("VE")
            ve_y_q[c] = p.cnt("VE")

        def emit_ct(c):
            y1 = q_y1b[c % 2]
            for i, (wsb, bT, dst, nop) in enumerate([(wdq16, bdqT, q_cq, NQP),
                                                     (wdkv16, bdkvT, q_ckv, NKP)]):
                for op in range(nop):
                    key = i * NQP + op
                    pp = pA if key % 2 == 0 else pB
                    wt = [("VE", ve_y_q[c]), ("SC", ct_ev_prev[key % 2])]
                    if c == 0 and key < 2:
                        wt.append(("LD", ld_w))
                    wt.append(("VE", combine_ve[key % 2]))
                    for dp in range(NDP):
                        p.emit("tensor",
                               lambda E, wsb=wsb, op=op, dp=dp, pp=pp, y1=y1:
                               E.matmul(pp[:, :],
                                        wsb[:, dp * 512 + op * 128:dp * 512 + op * 128 + 128],
                                        y1[:, dp * SCH:(dp + 1) * SCH],
                                        start=(dp == 0), stop=(dp == NDP - 1)),
                               wait=wt if dp == 0 else [], inc=["PE"])
                    pe_c = p.cnt("PE")
                    p.emit("scalar",
                           lambda E, dst=dst, bT=bT, op=op, pp=pp:
                           E.activation(dst[:, op * SCH:(op + 1) * SCH], pp[:, :], AF.Identity,
                                        bias=bT[:, op:op + 1]),
                           wait=[("PE", pe_c)], inc=["SC"])
                    ct_ev_prev[key % 2] = p.cnt("SC")
            sc_ct_q[c] = p.cnt("SC")

        def emit_v(c):
            cb = c % 2
            vb = [vga, vgb][cb]
            wvst = [("ST", vst_st[c - 2])] if c >= 2 else []
            v_evac = {}
            for tg in range(2):
                for j in range(4):
                    g = tg * 4 + j
                    reg = g % 2
                    wt = [("SC", sc_ct_q[c]), ("VE", pF_free[0]),
                          ("VE", v_evac.get(g - 2, 0))]
                    if c == 0 and g == 0:
                        wt.append(("LD", ld_w))
                    for kp in range(NKP):
                        p.emit("tensor",
                               lambda E, j=j, kp=kp, tg=tg, reg=reg: E.matmul(
                                   pF[:, reg * 512:(reg + 1) * 512],
                                   q_ckv[:, kp * SCH + j * 128:kp * SCH + j * 128 + 128],
                                   wuv16[:, kp * D + tg * 512:kp * D + tg * 512 + 512],
                                   start=(kp == 0), stop=(kp == NKP - 1)),
                               wait=wt if kp == 0 else [], inc=["PE"])
                    pe_v = p.cnt("PE")
                    wev = [("PE", pe_v)]
                    if g == 0:
                        wev += wvst + [("GP", gp_bvb)]
                    for trel in range(4):
                        p.emit("vector",
                               lambda E, j=j, tg=tg, reg=reg, vb=vb, trel=trel:
                               E.tensor_tensor(
                                   AP(vb, (tg * 4 + trel) * 520 + j * 130,
                                      [[NDP * 520, 128], [65, 2], [1, 64]]),
                                   AP(pF, reg * 512 + trel * 128,
                                      [[1024, 128], [64, 2], [1, 64]]),
                                   AP(bvb, tg * 512 + trel * 128,
                                      [[D, 128], [64, 2], [1, 64]]),
                                   ALU.add),
                               wait=wev if trel == 0 else [], inc=["VE"])
                    v_evac[g] = p.cnt("VE")
            ve_v = p.cnt("VE")
            pF_free[0] = ve_v
            for t8 in range(NDP):
                p.emit("sync",
                       lambda E, c=c, vb=vb, t8=t8: E.dma_start(
                           out=AP(v_h, t8 * 4160 + c * 520, [[NDP * 4160, 128], [1, 520]]),
                           in_=vb[:, t8 * 520:(t8 + 1) * 520]),
                       wait=[("VE", ve_v)] if t8 == 0 else [], inc=[("ST", 16)])
            vst_st[c] = p.cnt("ST")

        def emit_tsec(c, t):
            y1 = q_y1b[c % 2]
            sc_ct = sc_ct_q[c]
            tb = t % 2
            wt = [("SC", sc_ct), ("VE", combine_ve[0])]
            if c == 0 and t == 0:
                wt.append(("LD", ld_w))
            for qp in range(NQP):
                p.emit("tensor",
                       lambda E, qp=qp, t=t: E.matmul(
                           pA[:, :], wuq16[:, qp * D + t * 128:qp * D + t * 128 + 128],
                           q_cq[:, qp * SCH:(qp + 1) * SCH],
                           start=(qp == 0), stop=(qp == NQP - 1)),
                       wait=wt if qp == 0 else [], inc=["PE"])
            pe_qc = p.cnt("PE")
            wt = [("SC", sc_ct), ("SC", qr_sc_free[0])]
            for qp in range(NQP):
                p.emit("tensor",
                       lambda E, qp=qp, t=t: E.matmul(
                           pC[:, :], wqr16[:, qp * D + t * 128:qp * D + t * 128 + 128],
                           q_cq[:, qp * SCH:(qp + 1) * SCH],
                           start=(qp == 0), stop=(qp == NQP - 1)),
                       wait=wt if qp == 0 else [], inc=["PE"])
            pe_qr = p.cnt("PE")
            bias_q = bQcT if t > 0 else bqrT
            p.emit("scalar",
                   lambda E, t=t, bias_q=bias_q: E.activation(q_rs[:, :], pC[:, :], AF.Identity,
                                                              bias=bias_q[:, t:t + 1]),
                   wait=[("PE", pe_qr), ("VE", combine_ve[0])], inc=["SC"])
            sc_qrs = p.cnt("SC")
            qr_sc_free[0] = sc_qrs
            qb = [qsp0, qsp1][tb]
            wspill = ([("ST", qsp_st[t - 2])] if t >= 2 else
                      ([("ST", qsp_st[t + 6])] if (t + 6) in qsp_st else []))
            if t > 0:
                p.emit("vector",
                       lambda E, qb=qb: E.tensor_add(qb[:, :], q_rs[:, :], pA[:, :]),
                       wait=[("SC", sc_qrs), ("PE", pe_qc)] + wspill, inc=["VE"])
            else:
                p.emit("vector",
                       lambda E, qb=qb: E.scalar_tensor_tensor(
                           qb[:, :], pA[:, :], buqT[:, 0:1],
                           q_rs[:, :], ALU.add, ALU.add),
                       wait=[("SC", sc_qrs), ("PE", pe_qc)] + wspill,
                       inc=["VE"])
                p.emit("tensor",
                       lambda E: E.matmul(pE[0:32, 0:512], spsw[:, :], q_rs[0:32, :],
                                          start=True, stop=True),
                       wait=[("SC", sc_qrs), ("VE", tabfree_ve[0])], inc=["PE"])
                pe_sw = p.cnt("PE")
                p.emit("scalar",
                       lambda E: E.activation(swsb[:, :], pE[0:32, 0:512], AF.Identity),
                       wait=[("PE", pe_sw), ("SC", sc_tab_q[c])], inc=["SC"])
                ropefree_sc[0] = p.cnt("SC")
                sc_sw = p.cnt("SC")
                p.emit("vector", lambda E: E.tensor_mul(fx1[:, :], q_rs[0:32, :], Ctab[:, :]),
                       wait=[("SC", sc_sw)], inc=["VE"])
                p.emit("vector", lambda E: E.tensor_mul(fx2[:, :], swsb[:, :], Stab[:, :]),
                       inc=["VE"])
                p.emit("vector",
                       lambda E: E.scalar_tensor_tensor(fx3[:, :], fx1[:, :], buqT[0:32, 0:1],
                                                        fx2[:, :], ALU.add, ALU.add),
                       wait=[("VE", p.cnt("VE"))], inc=["VE"])
                p.emit("vector",
                       lambda E, qb=qb: E.tensor_add(qb[0:32, :], fx3[:, :], pA[0:32, :]),
                       wait=[("VE", p.cnt("VE"))], inc=["VE"])
            combine_ve[0] = p.cnt("VE")
            p.emit("sync",
                   lambda E, t=t, c=c, qb=qb: E.dma_start(
                       out=AP(qt_h, t * S + c * SCH, [[NDP * S, 128], [1, SCH]]),
                       in_=qb[:, :]),
                   wait=[("VE", combine_ve[0])], inc=[("ST", 16)])
            qsp_st[t] = p.cnt("ST")

            wt = [("SC", sc_ct), ("VE", combine_ve[1])]
            for kp in range(NKP):
                p.emit("tensor",
                       lambda E, kp=kp, t=t: E.matmul(
                           pB[:, :], wuk16[:, kp * D + t * 128:kp * D + t * 128 + 128],
                           q_ckv[:, kp * SCH:(kp + 1) * SCH],
                           start=(kp == 0), stop=(kp == NKP - 1)),
                       wait=wt if kp == 0 else [], inc=["PE"])
            pe_kc = p.cnt("PE")
            wt = [("VE", ve_y_q[c]), ("SC", kr_sc_free[0])]
            for dp in range(NDP):
                p.emit("tensor",
                       lambda E, dp=dp, t=t, y1=y1: E.matmul(
                           pD[:, :], wkr16[:, dp * D + t * 128:dp * D + t * 128 + 128],
                           y1[:, dp * SCH:(dp + 1) * SCH],
                           start=(dp == 0), stop=(dp == NDP - 1)),
                       wait=wt if dp == 0 else [], inc=["PE"])
            pe_kr = p.cnt("PE")
            bias_k = bKcT if t > 0 else bkrT
            p.emit("scalar",
                   lambda E, t=t, bias_k=bias_k: E.activation(k_rs[:, :], pD[:, :], AF.Identity,
                                                              bias=bias_k[:, t:t + 1]),
                   wait=[("PE", pe_kr), ("VE", combine_ve[1])], inc=["SC"])
            sc_krs = p.cnt("SC")
            kr_sc_free[0] = sc_krs
            kb = [ksp0, ksp1][tb]
            wspill = ([("ST", ksp_st[t - 2])] if t >= 2 else
                      ([("ST", ksp_st[t + 6])] if (t + 6) in ksp_st else []))
            if t > 0:
                p.emit("vector",
                       lambda E, kb=kb: E.tensor_add(kb[:, :], k_rs[:, :], pB[:, :]),
                       wait=[("SC", sc_krs), ("PE", pe_kc)] + wspill, inc=["VE"])
            else:
                p.emit("vector",
                       lambda E, kb=kb: E.scalar_tensor_tensor(
                           kb[:, :], pB[:, :], bukT[:, 0:1],
                           k_rs[:, :], ALU.add, ALU.add),
                       wait=[("SC", sc_krs), ("PE", pe_kc)] + wspill,
                       inc=["VE"])
                p.emit("tensor",
                       lambda E: E.matmul(pE[0:32, 0:512], spsw[:, :], k_rs[0:32, :],
                                          start=True, stop=True),
                       wait=[("SC", sc_krs), ("SC", ropefree_sc[0])], inc=["PE"])
                pe_sw = p.cnt("PE")
                p.emit("scalar",
                       lambda E: E.activation(swsb[:, :], pE[0:32, 0:512], AF.Identity),
                       wait=[("PE", pe_sw), ("VE", p.cnt("VE"))], inc=["SC"])
                ropefree_sc[0] = p.cnt("SC")
                sc_sw = p.cnt("SC")
                p.emit("vector", lambda E: E.tensor_mul(fx1[:, :], k_rs[0:32, :], Ctab[:, :]),
                       wait=[("SC", sc_sw)], inc=["VE"])
                p.emit("vector", lambda E: E.tensor_mul(fx2[:, :], swsb[:, :], Stab[:, :]),
                       inc=["VE"])
                p.emit("vector",
                       lambda E: E.scalar_tensor_tensor(fx3[:, :], fx1[:, :], bukT[0:32, 0:1],
                                                        fx2[:, :], ALU.add, ALU.add),
                       wait=[("VE", p.cnt("VE"))], inc=["VE"])
                p.emit("vector",
                       lambda E, kb=kb: E.tensor_add(kb[0:32, :], fx3[:, :], pB[0:32, :]),
                       wait=[("VE", p.cnt("VE"))], inc=["VE"])
            combine_ve[1] = p.cnt("VE")
            p.emit("sync",
                   lambda E, t=t, c=c, kb=kb: E.dma_start(
                       out=AP(kt_h, t * S + c * SCH, [[NDP * S, 128], [1, SCH]]),
                       in_=kb[:, :]),
                   wait=[("VE", combine_ve[1])], inc=[("ST", 16)])
            ksp_st[t] = p.cnt("ST")

        emit_tables(0)
        emit_ln_copies(0)
        emit_ln(0)
        emit_ln_y(0)
        for c in range(NCH):
            emit_ct(c)
            emit_v(c)
            for t in range(NDP):
                emit_tsec(c, t)
                if t == 1 and c + 1 < NCH:
                    emit_tables(c + 1)
                    emit_ln_copies(c + 1)
                if t == 2 and c + 1 < NCH:
                    emit_ln(c + 1)
                if t == 4 and c + 1 < NCH:
                    emit_ln_y(c + 1)
                if t == 3 and c + 2 < NCH:
                    emit_x1load(c + 2)
            p.emit("gpsimd", lambda E: E.nop(),
                   wait=[("ST", p.cnt("ST")), ("OUT", p.cnt("OUT"))])
        qkv_ld_gate = {t: max(qsp_st[t], ksp_st[t], vst_st[NCH - 1])
                       for t in range(NDP)}
        p.run("qkv")
        stk3.close()

    if debug_stage in ("qt", "kt", "v"):
        src = {"qt": qt_h, "kt": kt_h, "v": v_h}[debug_stage]
        return _finish_debug(nc, plan, counts, out_t, src, BF16,
                             NDP * S if debug_stage != "v" else NDP * 4160)

    if debug_stage == "x1aq":
        return _finish_debug(nc, plan, counts, out_t, x1t_h, F32, NDP * S)

    stk4 = ExitStack()
    ktt0 = stk4.enter_context(nc.sbuf_tensor("ktt0", [128, S], BF16))
    ktt1 = stk4.enter_context(nc.sbuf_tensor("ktt1", [128, S], BF16))
    qtt0 = stk4.enter_context(nc.sbuf_tensor("qtt0", [128, S], BF16))
    qtt1 = stk4.enter_context(nc.sbuf_tensor("qtt1", [128, S], BF16))
    vt0 = stk4.enter_context(nc.sbuf_tensor("vt0", [128, 4160], BF16))
    vt1 = stk4.enter_context(nc.sbuf_tensor("vt1", [128, 4160], BF16))
    pr0 = stk4.enter_context(nc.sbuf_tensor("pr0", [128, 1024], BF16))
    pr1 = stk4.enter_context(nc.sbuf_tensor("pr1", [128, 1024], BF16))
    pr2 = stk4.enter_context(nc.sbuf_tensor("pr2", [128, 1024], BF16))
    pr3 = stk4.enter_context(nc.sbuf_tensor("pr3", [128, 1024], BF16))
    rbx0 = stk4.enter_context(nc.sbuf_tensor("rbx0", [64, 1024], F32))
    rbx1 = stk4.enter_context(nc.sbuf_tensor("rbx1", [64, 1024], F32))
    rcp0 = stk4.enter_context(nc.sbuf_tensor("rcp0", [1, 1024], F32))
    rcp1 = stk4.enter_context(nc.sbuf_tensor("rcp1", [1, 1024], F32))
    avst0 = stk4.enter_context(nc.sbuf_tensor("avst0", [65, 1024], F32))
    avst1 = stk4.enter_context(nc.sbuf_tensor("avst1", [65, 1024], F32))
    qz0 = stk4.enter_context(nc.sbuf_tensor("qz0", [128, S], BF16))
    qz1 = stk4.enter_context(nc.sbuf_tensor("qz1", [128, S], BF16))
    qz2 = stk4.enter_context(nc.sbuf_tensor("qz2", [128, S], BF16))
    qz3 = stk4.enter_context(nc.sbuf_tensor("qz3", [128, S], BF16))
    ao0 = stk4.enter_context(nc.sbuf_tensor("ao0", [128, S], BF16))
    ao1 = stk4.enter_context(nc.sbuf_tensor("ao1", [128, S], BF16))
    if True:
        p = plan()
        ktt = [ktt0, ktt1]
        qtt = [qtt0, qtt1]
        vtt = [vt0, vt1]
        aot = [ao0, ao1]
        probs = [pr0, pr1, pr2, pr3]
        rbxb = [rbx0, rbx1]
        rcpb = [rcp0, rcp1]
        ld_t = {}
        core_done_pe = {}
        ao_st = {}
        norm_ve = [0]
        score_free = {0: 0, 1: 0, 2: 0}
        probs_free_pe = [0, 0, 0, 0]
        pv_free_ve = [0]
        kg = [0]
        it_ctr = [0]
        qz_ve = {}
        qz_pe_free = [0, 0]
        rcp_gp = [0, 0]
        mul_gp = [0, 0]

        def emit_loads(t):
            tb = t % 2
            wld = [("ST", qkv_ld_gate[t])]
            if t >= 2:
                wld.append(("PE", core_done_pe[t - 2]))
            p.emit("sync",
                   lambda E, t=t, tb=tb: E.dma_start(
                       out=ktt[tb][:, :], in_=AP(kt_h, t * S, [[NDP * S, 128], [1, S]])),
                   wait=wld, inc=[("LD", 16)])
            p.emit("sync",
                   lambda E, t=t, tb=tb: E.dma_start(
                       out=qtt[tb][:, :], in_=AP(qt_h, t * S, [[NDP * S, 128], [1, S]])),
                   inc=[("LD", 16)])
            p.emit("sync",
                   lambda E, t=t, tb=tb: E.dma_start(
                       out=vtt[tb][:, :], in_=AP(v_h, t * 4160, [[NDP * 4160, 128], [1, 4160]])),
                   inc=[("LD", 16)])
            ld_t[t] = p.cnt("LD")

        def emit_qz(t):
            tb = t % 2
            qza = [qz0, qz1] if tb == 0 else [qz2, qz3]
            wqz = [("LD", ld_t[t])]
            if t >= 2:
                wqz.append(("PE", qz_pe_free[tb]))
            p.emit("vector", lambda E, tb=tb, qza=qza: E.tensor_copy(qza[0][0:64, :],
                                                                    qtt[tb][0:64, :]),
                   wait=wqz, inc=["VE"])
            p.emit("vector", lambda E, tb=tb, qza=qza: E.tensor_copy(qza[1][64:128, :],
                                                                     qtt[tb][64:128, :]),
                   inc=["VE"])
            qz_ve[t] = p.cnt("VE")

        for qzz, half in [(qz0, 1), (qz1, 0), (qz2, 1), (qz3, 0)]:
            p.emit("vector",
                   lambda E, qzz=qzz, half=half: E.memset(
                       qzz[half * 64:half * 64 + 64, :], 0.0),
                   inc=["VE"])
        emit_loads(0)
        emit_qz(0)
        for t in range(NDP):
            tb = t % 2
            qza = [qz0, qz1] if tb == 0 else [qz2, qz3]
            for hh in range(2):
                for qc in range(4):
                    it = it_ctr[0]
                    it_ctr[0] += 1
                    ip = it % 2
                    avb = [avst0, avst1][ip]
                    exp_at = {}

                    def emit_av(k, tb=tb, hh=hh):
                        wav = [("SC", exp_at[k])]
                        if k == 0:
                            wav.append(("VE", pv_free_ve[0]))
                        for half in range(2):
                            pav = [pA, pB][half]
                            p.emit("tensor",
                                   lambda E, tb=tb, hh=hh, k=k, pav=pav:
                                   E.matmul(pav[0:65, :],
                                            vtt[tb][:, k * 130 + hh * 65:k * 130 + hh * 65 + 65],
                                            probs[k % 4][:, half * 512:(half + 1) * 512],
                                            start=(k == 0), stop=(k == 31)),
                                   wait=wav if half == 0 else [], inc=["PE"])
                        probs_free_pe[k % 4] = p.cnt("PE")

                    for k in range(32):
                        reg = kg[0] % 3
                        kg[0] += 1
                        sc = [pE, pF, pG][reg]
                        wt = [("SC", score_free[reg])]
                        if k == 0:
                            wt += [("LD", ld_t[t]), ("VE", qz_ve[t])]
                        qz = qza[hh]
                        for half in range(2):
                            p.emit("tensor",
                                   lambda E, tb=tb, qc=qc, k=k, sc=sc, half=half, qz=qz:
                                   E.matmul(sc[:, half * 512:(half + 1) * 512],
                                            ktt[tb][:, k * 128:(k + 1) * 128],
                                            qz[:, qc * 1024 + half * 512:
                                               qc * 1024 + half * 512 + 512],
                                            start=True, stop=True),
                                   wait=wt if half == 0 else [], inc=["PE"])
                        pe_s = p.cnt("PE")
                        wex = [("PE", pe_s), ("PE", probs_free_pe[k % 4])]
                        p.emit("scalar",
                               lambda E, k=k, sc=sc: E.activation(probs[k % 4][:, :], sc[:, :],
                                                                  AF.Exp, scale=INV_SCALE),
                               wait=wex, inc=["SC"])
                        score_free[reg] = p.cnt("SC")
                        exp_at[k] = p.cnt("SC")
                        if k >= 1:
                            emit_av(k - 1)
                    emit_av(31)
                    pe_av = p.cnt("PE")
                    p.emit("vector",
                           lambda E, avb=avb: E.tensor_copy(avb[:, 0:512], pA[0:65, :]),
                           wait=[("PE", pe_av)], inc=["VE"])
                    p.emit("vector",
                           lambda E, avb=avb: E.tensor_copy(avb[:, 512:1024], pB[0:65, :]),
                           inc=["VE"])
                    pv_free_ve[0] = p.cnt("VE")
                    p.emit("vector",
                           lambda E, avb=avb, ip=ip: E.reciprocal(rcpb[ip][0:1, :],
                                                                  avb[64:65, :]),
                           wait=[("GP", rcp_gp[ip])], inc=["VE"])
                    ve_rc = p.cnt("VE")
                    p.emit("gpsimd",
                           lambda E, ip=ip: E.dma_start(
                               out=rbxb[ip][:, :],
                               in_=AP(rcpb[ip], 0, [[1024, 1], [0, 64], [1, 1024]])),
                           wait=[("VE", ve_rc)], inc=[("GP", 16)])
                    rcp_gp[ip] = p.cnt("GP")
                    mul_gp[ip] = p.cnt("GP")
                    wao = ([("ST", ao_st[t - 2])]
                           if (t >= 2 and hh == 0 and qc == 0) else [])
                    p.emit("vector",
                           lambda E, avb=avb, ip=ip, tb=tb, hh=hh, qc=qc: E.tensor_mul(
                               aot[tb][hh * 64:(hh + 1) * 64, qc * 1024:qc * 1024 + 512],
                               avb[0:64, 0:512], rbxb[ip][:, 0:512]),
                           wait=[("GP", mul_gp[ip])] + wao, inc=["VE"])
                    p.emit("vector",
                           lambda E, avb=avb, ip=ip, tb=tb, hh=hh, qc=qc: E.tensor_mul(
                               aot[tb][hh * 64:(hh + 1) * 64,
                                       qc * 1024 + 512:qc * 1024 + 1024],
                               avb[0:64, 512:1024], rbxb[ip][:, 512:1024]),
                           inc=["VE"])
                    norm_ve[0] = p.cnt("VE")
                    if hh == 0 and qc == 0 and t + 1 < NDP:
                        emit_loads(t + 1)
                    if hh == 1 and qc == 0 and t + 1 < NDP:
                        emit_qz(t + 1)
            core_done_pe[t] = p.cnt("PE")
            qz_pe_free[tb] = p.cnt("PE")
            p.emit("sync",
                   lambda E, t=t, tb=tb: E.dma_start(
                       out=AP(ao_h, t * S, [[NDP * S, 128], [1, S]]), in_=aot[tb][:, :]),
                   wait=[("VE", norm_ve[0])], inc=[("ST", 16)])
            ao_st[t] = p.cnt("ST")
            p.emit("gpsimd", lambda E: E.nop(),
                   wait=[("ST", p.cnt("ST")), ("OUT", p.cnt("OUT"))])
        ao_st_all[0] = p.cnt("ST")
        p.run("core")
        stk4.close()

    if debug_stage == "ao":
        return _finish_debug(nc, plan, counts, out_t, ao_h, BF16, NDP * S)
    if debug_stage == "x1ac":
        return _finish_debug(nc, plan, counts, out_t, x1t_h, F32, NDP * S)

    stk56 = ExitStack()
    _w1qb = stk56.enter_context(nc.sbuf_tensor("w1qb", [128, NDP * DFF], BF16))
    _w2qb = stk56.enter_context(nc.sbuf_tensor("w2qb", [128, NFT * D], BF16))
    stk5 = ExitStack()
    wo16 = stk5.enter_context(nc.sbuf_tensor("wo16", [128, NDP * D], BF16))
    aoc0 = stk5.enter_context(nc.sbuf_tensor("aoc0", [128, NDP * SCH], BF16))
    aoc1 = stk5.enter_context(nc.sbuf_tensor("aoc1", [128, NDP * SCH], BF16))
    ox0 = stk5.enter_context(nc.sbuf_tensor("ox0", [128, NDP * SCH], F32))
    ox1 = stk5.enter_context(nc.sbuf_tensor("ox1", [128, NDP * SCH], F32))
    oet0 = stk5.enter_context(nc.sbuf_tensor("oet0", [128, 512], F32))
    oet1 = stk5.enter_context(nc.sbuf_tensor("oet1", [128, 512], F32))
    if True:
        p = plan()
        p.emit("gpsimd",
               lambda E: E.dma_start(out=wo16[:, :],
                                     in_=AP(wins["wo"], 0, [[D, 128], [128 * D, NDP], [1, D]])),
               inc=[("LW", 16)])
        ld_wo = p.cnt("LW")
        wq_pref = [None]
        aoc = [aoc0, aoc1]
        oxc = [ox0, ox1]
        oet = [oet0, oet1]
        st_m = {}
        add_ve = {}
        ev_sc = {}
        ev_last = [0, 0]
        add_last = [0, 0]
        ld_ao = {}
        ld_ox = {}

        def emit_oload(c):
            cb = c % 2
            w = [("ST", st_m[c - 2])] if c >= 2 else []
            p.emit("sync",
                   lambda E, c=c, cb=cb: E.dma_start(
                       out=oxc[cb][:, :],
                       in_=AP(x1t_h, c * SCH, [[NDP * S, 128], [S, NDP], [1, SCH]])),
                   wait=w, inc=[("LA", 16)])
            p.emit("sync",
                   lambda E, c=c, cb=cb: E.dma_start(
                       out=aoc[cb][:, :],
                       in_=AP(ao_h, c * SCH, [[NDP * S, 128], [S, NDP], [1, SCH]])),
                   wait=[("ST", ao_st_all[0])] if c < 2 else [], inc=[("LA", 16)])
            ld_ao[c] = p.cnt("LA")
            ld_ox[c] = p.cnt("LA")

        emit_oload(0)
        emit_oload(1)
        p.emit("gpsimd", lambda E: E.dma_start(out=_w1qb[:, :], in_=w1q_h.ap()),
               wait=[("ST", wq_st_mark[0]), ("LA", ld_ao[1])], inc=[("LD2", 16)])
        p.emit("gpsimd", lambda E: E.dma_start(out=_w2qb[:, :], in_=w2q_h.ap()),
               inc=[("LD2", 16)])
        ld_wq2 = p.cnt("LD2")
        for c in range(NCH):
            cb = c % 2
            for dp in range(NDP):
                pp = pA if dp % 2 == 0 else pB
                wt = [("LA", ld_ao[c] if c >= 2 else ld_ao[1]),
                      ("SC", ev_last[dp % 2])]
                if c == 0 and dp == 0:
                    wt.append(("LW", ld_wo))
                for kp in range(NDP):
                    p.emit("tensor",
                           lambda E, dp=dp, kp=kp, pp=pp, cb=cb:
                           E.matmul(pp[:, :],
                                    wo16[:, kp * D + dp * 128:kp * D + dp * 128 + 128],
                                    aoc[cb][:, kp * SCH:(kp + 1) * SCH],
                                    start=(kp == 0), stop=(kp == NDP - 1)),
                           wait=wt if kp == 0 else [], inc=["PE"])
                pe_o = p.cnt("PE")
                eb = oet[dp % 2]
                p.emit("scalar",
                       lambda E, dp=dp, pp=pp, eb=eb:
                       E.activation(eb[:, :], pp[:, :], AF.Identity, bias=boT[:, dp:dp + 1]),
                       wait=[("PE", pe_o), ("VE", add_last[dp % 2])],
                       inc=["SC"])
                ev_last[dp % 2] = p.cnt("SC")
                ev_sc[dp] = p.cnt("SC")
                p.emit("vector",
                       lambda E, dp=dp, cb=cb, eb=eb:
                       E.tensor_add(oxc[cb][:, dp * SCH:(dp + 1) * SCH],
                                    oxc[cb][:, dp * SCH:(dp + 1) * SCH], eb[:, :]),
                       wait=[("SC", ev_sc[dp])] +
                            ([("LA", ld_ox[c] if c >= 2 else ld_ao[1])]
                             if dp == 0 else []),
                       inc=["VE"])
                add_last[dp % 2] = p.cnt("VE")
                add_ve[dp] = p.cnt("VE")
            ve_x2 = p.cnt("VE")
            p.emit("scalar",
                   lambda E, c=c, cb=cb: E.dma_start(
                       out=AP(x2t_h, c * SCH, [[NDP * S, 128], [S, NDP], [1, SCH]]),
                       in_=oxc[cb][:, :]),
                   wait=[("VE", ve_x2)], inc=[("ST", 16)])
            st_m[c] = p.cnt("ST")
            if c + 2 < NCH:
                emit_oload(c + 2)
            p.emit("gpsimd", lambda E: E.nop(),
                   wait=[("ST", p.cnt("ST")), ("OUT", p.cnt("OUT"))])
        p.run("oproj")
        stk5.close()

    if debug_stage == "x2":
        return _finish_debug(nc, plan, counts, out_t, x2t_h, F32, NDP * S)

    stk6 = ExitStack()
    g_xt0 = stk6.enter_context(nc.sbuf_tensor("g_xt0", [128, NDP * SCH], F32))
    g_xt1 = stk6.enter_context(nc.sbuf_tensor("g_xt1", [128, NDP * SCH], F32))
    g_xb0 = stk6.enter_context(nc.sbuf_tensor("g_xb0", [128, SCH], BF16))
    g_xb1 = stk6.enter_context(nc.sbuf_tensor("g_xb1", [128, SCH], BF16))
    g_xq0 = stk6.enter_context(nc.sbuf_tensor("g_xq0", [128, SCH], BF16))
    g_xq1 = stk6.enter_context(nc.sbuf_tensor("g_xq1", [128, SCH], BF16))
    g_yb = stk6.enter_context(nc.sbuf_tensor("g_yb", [128, NDP * SCH], BF16))
    g_ht = stk6.enter_context(nc.sbuf_tensor("g_ht", [128, NFT * (SCH // 2)], BF16))
    g_et0 = stk6.enter_context(nc.sbuf_tensor("g_et0", [128, 512], F32))
    g_et1 = stk6.enter_context(nc.sbuf_tensor("g_et1", [128, 512], F32))
    g_tmp = stk6.enter_context(nc.sbuf_tensor("g_tmp", [128, 512], F32))
    g_onat = stk6.enter_context(nc.sbuf_tensor("g_onat", [128, 2 * D], F32))
    if True:
        w1q_sb[0] = _w1qb
        w2q_sb[0] = _w2qb
        p = plan()
        ld_wq = ld_wq2
        trev_marks = {}
        dacc_sc = {}
        bufs = dict(xtb=[g_xt0, g_xt1], xb=[g_xb0, g_xb1], xsq=[g_xq0, g_xq1],
                    yb=g_yb, ht=g_ht, tmp512=g_tmp, etmp=[g_et0, g_et1],
                    onat=g_onat, dbg_x2chk=(debug_stage == "x2chk"))
        ffn_phase(p, "x2t", "out", bufs, [("LD2", ld_wq)] * NFT, [("LD2", ld_wq)] * NDP)
        p.emit("gpsimd", lambda E: E.nop(), wait=[("OUT", p.cnt("OUT"))])
        p.run("ffn2")
        stk6.close()
        stk56.close()

    if debug_stage == "x2chk":
        return _finish_debug(nc, plan, counts, out_t, ao_h, BF16, NDP * S)

    ctx.close()
    return nc


def _finish_debug(nc, plan, counts, out_t, src, dt, width):
    w = min(width, S * D // 128)
    p = plan()
    if dt == F32:
        p.emit("sync",
               lambda E: E.dma_start(out=AP(out_t, 0, [[w, 128], [1, w]]),
                                     in_=AP(src, 0, [[width, 128], [1, w]])),
               inc=[("OUT", 16)])
    else:
        p.emit("gpsimd",
               lambda E: E.dma_start(out=AP(out_t, 0, [[w, 128], [1, w]]),
                                     in_=AP(src, 0, [[width, 128], [1, w]])),
               inc=[("OUT", 16)])
    p.emit("gpsimd" if dt != F32 else "sync", lambda E: E.nop(),
           wait=[("OUT", counts["OUT"])])
    p.run("dbg")
    return nc


_NC_CACHE = {}


def _get_nc():
    key = DEBUG_STAGE
    if key not in _NC_CACHE:
        _NC_CACHE[key] = build_program(debug_stage=DEBUG_STAGE)
    return _NC_CACHE[key]


def kernel(**inputs):
    x = np.ascontiguousarray(np.asarray(inputs["L0_attn_x"], dtype=np.float32))
    B = x.shape[0]
    names = {
        "w1": "L0_ffn_W1", "b1": "L0_ffn_b1", "w2": "L0_ffn_W2", "b2": "L0_ffn_b2",
        "wdq": "L0_attn_Wd_q_w", "bdq": "L0_attn_Wd_q_b",
        "wuq": "L0_attn_Wu_q_w", "buq": "L0_attn_Wu_q_b",
        "wqr": "L0_attn_Wq_r_w", "bqr": "L0_attn_Wq_r_b",
        "wkr": "L0_attn_Wk_r_w", "bkr": "L0_attn_Wk_r_b",
        "wdkv": "L0_attn_Wd_kv_w", "bdkv": "L0_attn_Wd_kv_b",
        "wuk": "L0_attn_Wu_k_w", "buk": "L0_attn_Wu_k_b",
        "wuv": "L0_attn_Wu_v_w", "buv": "L0_attn_Wu_v_b",
        "wo": "L0_attn_Wo_w", "bo": "L0_attn_Wo_b",
    }
    shared = {k: np.ascontiguousarray(np.asarray(inputs[v], dtype=np.float32))
              for k, v in names.items()}
    shared["eye32"] = np.eye(128, dtype=np.float32)
    psw = np.zeros((32, 32), dtype=np.float32)
    for i in range(16):
        psw[2 * i + 1, 2 * i] = -1.0
        psw[2 * i, 2 * i + 1] = 1.0
    shared["psw"] = psw

    nc = _get_nc()
    in_maps = [dict(shared, x=np.ascontiguousarray(x[b])) for b in range(B)]
    res = run_bass_kernel_spmd(nc, in_maps, core_ids=list(range(B)))
    out = np.stack([np.asarray(res.results[b]["out"]) for b in range(B)], axis=0)
    return out.astype(np.float32)


# version 52
# speedup vs baseline: 1.0100x; 1.0100x over previous
import math
import numpy as np
import ml_dtypes
import concourse.bass as bass
import concourse.mybir as mybir
from concourse.bass_utils import run_bass_kernel_spmd
from contextlib import ExitStack

F32 = mybir.dt.float32
BF16 = mybir.dt.bfloat16
AF = mybir.ActivationFunctionType
ALU = mybir.AluOpType
AP = bass.AP
AXX = mybir.AxisListType.X

D = 1024
S = 4096
DQ = 512
DKV = 512
H = 16
HD = 64
DFF = 4096
NDP = D // 128
NQP = DQ // 128
NKP = DKV // 128
NFT = DFF // 128
SCH = 512
NCH = S // SCH
EPS = 1e-5
PI = float(np.pi)
LN1E4 = float(np.log(10000.0))
INV_SCALE = float(np.float32(1.0) / np.float32(np.sqrt(np.float32(H + DKV + DQ))))

STRICT = False
DEBUG_STAGE = None


class Plan:
    ENGINES = ("sync", "tensor", "scalar", "vector", "gpsimd")

    def __init__(self, nc, sems, counts):
        self.nc = nc
        self.sems = sems
        self.count = counts
        self.ops = {e: [] for e in self.ENGINES}

    def emit(self, eng, fn, wait=(), inc=()):
        if STRICT:
            wait = [(s, self.count[s]) for s in self.sems]
        for s, v in wait:
            assert v <= self.count[s], ("wait ahead of plan", s, v, self.count[s])
        incs = []
        for item in inc:
            s, n = item if isinstance(item, tuple) else (item, 1)
            self.count[s] += n
            incs.append((s, n))
        self.ops[eng].append((fn, [(s, int(v)) for s, v in wait], incs))

    def cnt(self, s):
        return self.count[s]

    def run(self, name=None):
        nc = self.nc
        sems = self.sems
        with nc.Block(name) as blk:
            for eng in self.ENGINES:
                if not self.ops[eng]:
                    continue
                ops = self.ops[eng]

                def body(E, ops=ops):
                    hwm = {}
                    for fn, waits, incs in ops:
                        for s, v in waits:
                            if hwm.get(s, -1) < v:
                                E.wait_ge(sems[s], v)
                                hwm[s] = v
                        inst = fn(E)
                        for s, n in incs:
                            inst.then_inc(sems[s], n)
                getattr(blk, eng)(body)
        self.ops = {e: [] for e in self.ENGINES}


def build_program(debug_stage=None):
    nc = bass.Bass("TRN2", target_bir_lowering=False)

    x_in = nc.dram_tensor("x", [S, D], F32, kind="ExternalInput")
    wins = {}
    for nm, shp in [
        ("w1", [DFF, D]), ("b1", [DFF]), ("w2", [D, DFF]), ("b2", [D]),
        ("wdq", [D, DQ]), ("bdq", [DQ]), ("wuq", [DQ, D]), ("buq", [D]),
        ("wqr", [DQ, D]), ("bqr", [D]), ("wkr", [D, D]), ("bkr", [D]),
        ("wdkv", [D, DKV]), ("bdkv", [DKV]), ("wuk", [DKV, D]), ("buk", [D]),
        ("wuv", [DKV, D]), ("buv", [D]), ("wo", [D, D]), ("bo", [D]),
    ]:
        wins[nm] = nc.dram_tensor(nm, shp, F32, kind="ExternalInput")
    eye32 = nc.dram_tensor("eye32", [128, 128], F32, kind="ExternalInput")
    psw_in = nc.dram_tensor("psw", [32, 32], F32, kind="ExternalInput")

    out_t = nc.dram_tensor("out", [S, D], F32, kind="ExternalOutput")

    x1t_h = nc.dram_tensor("x1t_h", [128, NDP * S], F32, kind="Internal")
    x2t_h = nc.dram_tensor("x2t_h", [128, NDP * S], F32, kind="Internal")
    qt_h = nc.dram_tensor("qt_h", [128, NDP * S], BF16, kind="Internal")
    kt_h = nc.dram_tensor("kt_h", [128, NDP * S], BF16, kind="Internal")
    v_h = nc.dram_tensor("v_h", [128, NDP * 32 * 130], BF16, kind="Internal")
    ao_h = nc.dram_tensor("ao_h", [128, NDP * S], BF16, kind="Internal")
    w1q_h = nc.dram_tensor("w1q_h", [128, NDP * DFF], BF16, kind="Internal")
    w2q_h = nc.dram_tensor("w2q_h", [128, NFT * D], BF16, kind="Internal")

    ctx = ExitStack()
    sems, counts = {}, {}
    for s_ in ["LD", "LD2", "LW", "LA", "ST", "PE", "SC", "VE", "GP", "OUT"]:
        sems[s_] = ctx.enter_context(nc.semaphore(s_))
        counts[s_] = 0

    def plan():
        return Plan(nc, sems, counts)

    def sb(name, shape, dt=F32):
        return ctx.enter_context(nc.sbuf_tensor(name, shape, dt))

    def ps(name, shape, dt=F32):
        return ctx.enter_context(nc.psum_tensor(name, shape, dt))

    ones16c = sb("ones16c", [128, 1], BF16)
    ones_colf = sb("ones_colf", [128, 1], F32)
    ones32r = sb("ones32r", [1, 128], F32)
    ones65r = sb("ones65r", [65, 128], F32)
    ones16r = sb("ones16r", [1, 512], BF16)
    epst = sb("epst", [1, 1], F32)
    halfpi32 = sb("halfpi32", [32, 1], F32)
    seye32 = sb("seye32", [128, 128], F32)
    seye16 = sb("seye16", [128, 128], BF16)
    spsw = sb("spsw", [32, 32], BF16)
    thdup = sb("thdup", [1, 32], F32)
    b1T = sb("b1T", [128, 32], F32)
    b2hT = sb("b2hT", [128, 8], F32)
    bdqT = sb("bdqT", [128, 4], F32)
    bdkvT = sb("bdkvT", [128, 4], F32)
    buqT = sb("buqT", [128, 8], F32)
    bqrT = sb("bqrT", [128, 8], F32)
    bQcT = sb("bQcT", [128, 8], F32)
    bukT = sb("bukT", [128, 8], F32)
    bkrT = sb("bkrT", [128, 8], F32)
    bKcT = sb("bKcT", [128, 8], F32)
    boT = sb("boT", [128, 8], F32)
    rows3 = sb("rows3", [65, 512], F32)
    row_a = rows3[0:1, :]
    row_b = rows3[32:33, :]
    row_c = rows3[64:65, :]


    pA = ps("pA", [128, 512])
    pB = ps("pB", [128, 512])
    pG = ps("pG", [128, 1024])
    pE = ps("pE", [128, 1024])
    pF = ps("pF", [128, 1024])
    pC = pG[:, 0:512]
    pD = pG[:, 512:1024]

    stk0 = ExitStack()
    braw = stk0.enter_context(nc.sbuf_tensor("braw", [32, 9 * 128], F32))
    p = plan()
    p.emit("sync", lambda E: E.dma_start(out=seye32[:, :], in_=eye32.ap()), inc=[("LD", 16)])
    p.emit("gpsimd", lambda E: E.dma_start(out=seye16[:, :], in_=eye32.ap()), inc=[("LD", 16)])
    p.emit("gpsimd", lambda E: E.dma_start(out=spsw[:, :], in_=psw_in.ap()), inc=[("LD", 16)])
    for t, val in [(ones16c, 1.0), (ones_colf, 1.0), (ones32r, 1.0), (ones65r, 1.0),
                   (ones16r, 1.0), (epst, EPS), (halfpi32, 0.5 * PI)]:
        p.emit("vector", lambda E, t=t, val=val: E.memset(t[:, :], val), inc=["VE"])
    p.emit("gpsimd", lambda E: E.iota(row_a[0:1, 0:16], [[1, 16]], channel_multiplier=0,
                                      allow_small_or_imprecise_dtypes=True), inc=["GP"])
    p.emit("scalar", lambda E: E.activation(row_b[0:1, 0:16], row_a[0:1, 0:16], AF.Exp,
                                            scale=-2.0 * LN1E4),
           wait=[("GP", p.cnt("GP"))], inc=["SC"])
    p.emit("vector", lambda E: E.tensor_scalar_mul(row_c[0:1, 0:16], row_b[0:1, 0:16], 1.0 / 512.0),
           wait=[("SC", p.cnt("SC"))], inc=["VE"])
    ve_th = p.cnt("VE")
    p.emit("vector", lambda E: E.tensor_copy(AP(thdup, 0, [[32, 1], [2, 16]]), row_c[0:1, 0:16]),
           wait=[("VE", ve_th)], inc=["VE"])
    p.emit("vector", lambda E: E.tensor_copy(AP(thdup, 1, [[32, 1], [2, 16]]), row_c[0:1, 0:16]),
           wait=[("VE", ve_th)], inc=["VE"])

    bias_jobs = [
        ("b1", b1T, 32, None), ("b2", b2hT, 8, 0.5),
        ("bdq", bdqT, 4, None), ("bdkv", bdkvT, 4, None),
        ("buq", buqT, 8, None), ("bqr", bqrT, 8, None),
        ("buk", bukT, 8, None), ("bkr", bkrT, 8, None),
        ("bo", boT, 8, None),
    ]
    for bi, (nm, dst, nb, scl) in enumerate(bias_jobs):
        h = wins[nm]
        p.emit("sync",
               lambda E, h=h, nb=nb, bi=bi: E.dma_start(
                   out=braw[0:nb, bi * 128:(bi + 1) * 128],
                   in_=AP(h, 0, [[128, nb], [1, 128]])),
               inc=[("LD", 16)])
    ld_b = p.cnt("LD")
    ve0 = p.cnt("VE")
    bevac = {}
    for bi, (nm, dst, nb, scl) in enumerate(bias_jobs):
        pp = pA if bi % 2 == 0 else pB
        wt = [("LD", ld_b), ("VE", bevac.get(bi - 2, ve0))]
        p.emit("tensor",
               lambda E, nb=nb, bi=bi, pp=pp: E.transpose(
                   pp[:, 0:nb], braw[0:nb, bi * 128:(bi + 1) * 128],
                   seye32[0:nb, 0:nb]),
               wait=wt, inc=["PE"])
        pe_tr = p.cnt("PE")
        if scl is None:
            p.emit("vector",
                   lambda E, dst=dst, nb=nb, pp=pp: E.tensor_copy(dst[:, 0:nb], pp[:, 0:nb]),
                   wait=[("PE", pe_tr)], inc=["VE"])
        else:
            p.emit("vector",
                   lambda E, dst=dst, nb=nb, scl=scl, pp=pp: E.tensor_scalar_mul(
                       dst[:, 0:nb], pp[:, 0:nb], scl),
                   wait=[("PE", pe_tr)], inc=["VE"])
        bevac[bi] = p.cnt("VE")
    p.emit("vector", lambda E: E.tensor_add(bQcT[:, :], buqT[:, :], bqrT[:, :]),
           wait=[("VE", p.cnt("VE"))], inc=["VE"])
    p.emit("vector", lambda E: E.tensor_add(bKcT[:, :], bukT[:, :], bkrT[:, :]),
           wait=[("VE", p.cnt("VE"))], inc=["VE"])
    p.run("p0")
    stk0.close()

    w1q_sb = [None]
    w2q_sb = [None]
    wq_st_mark = [0]
    ffn1_st = {}
    ao_st_all = [0]

    def q_geom(n_row_tiles, row_cols):
        PC = 2048
        if row_cols <= PC:
            tpp, ncc = PC // row_cols, 1
        else:
            tpp, ncc = 1, row_cols // PC
        return tpp, ncc, (n_row_tiles // tpp) * ncc

    def q_piece_ap(h, row_cols, tpp, ncc, pi):
        PC = 2048
        if ncc == 1:
            return AP(h, pi * tpp * 128 * row_cols,
                      [[row_cols, 128], [128 * row_cols, tpp], [1, row_cols]])
        ti, cc = divmod(pi, ncc)
        return AP(h, ti * 128 * row_cols + cc * PC,
                  [[row_cols, 128], [128 * row_cols, 1], [1, PC]])

    def q_piece_blocks(row_cols, tpp, ncc, pi):
        PC = 2048
        out = []
        if ncc == 1:
            for tl in range(tpp):
                for sub in range(row_cols // 128):
                    out.append((pi * tpp + tl, sub, tl * row_cols + sub * 128))
        else:
            ti, cc = divmod(pi, ncc)
            for s in range(PC // 128):
                out.append((ti, cc * (PC // 128) + s, s * 128))
        return out

    def quant_pass1(p, wname, wpc, eng, gcol, n_row_tiles, row_cols):
        h = wins[wname]
        tpp, ncc, npieces = q_geom(n_row_tiles, row_cols)
        ldsem = "LD" if eng == "sync" else "LD2"
        gred = {}
        for pi in range(npieces):
            b = pi % 2
            w = [("VE", gred[pi - 2])] if pi >= 2 else []
            p.emit(eng,
                   lambda E, pi=pi, b=b: E.dma_start(
                       out=wpc[b][:, :], in_=q_piece_ap(h, row_cols, tpp, ncc, pi)),
                   wait=w, inc=[(ldsem, 16)])
            p.emit("vector",
                   lambda E, pi=pi, b=b: E.reduce_sum(gtmp[:, gcol + pi:gcol + pi + 1],
                                                      wpc[b][:, :], axis=AXX,
                                                      apply_absolute_value=True),
                   wait=[(ldsem, p.cnt(ldsem))], inc=["VE"])
            gred[pi] = p.cnt("VE")
        return gred

    def quant_gamma(p, gcol, npieces, nelem, rcol, pcol, thrcol):
        p.emit("tensor",
               lambda E: E.matmul(pD[0:1, pcol:pcol + npieces], ones_colf[:, :],
                                  gtmp[:, gcol:gcol + npieces], start=True, stop=True),
               wait=[("VE", p.cnt("VE"))], inc=["PE"])
        p.emit("vector",
               lambda E: E.reduce_sum(row_a[0:1, rcol:rcol + 1],
                                      pD[0:1, pcol:pcol + npieces], axis=AXX),
               wait=[("PE", p.cnt("PE"))], inc=["VE"])
        p.emit("vector",
               lambda E: E.tensor_scalar(row_a[0:1, rcol + 1:rcol + 2],
                                         row_a[0:1, rcol:rcol + 1],
                                         0.5 / nelem, 0.5 * EPS, ALU.mult, ALU.add),
               wait=[("VE", p.cnt("VE"))], inc=["VE"])
        p.emit("vector",
               lambda E: E.tensor_scalar_mul(row_a[0:1, rcol + 2:rcol + 3],
                                             row_a[0:1, rcol + 1:rcol + 2], -1.0),
               wait=[("VE", p.cnt("VE"))], inc=["VE"])
        ve_thr = p.cnt("VE")
        p.emit("tensor",
               lambda E: E.matmul(pD[:, pcol + 100:pcol + 102], ones32r[0:1, :],
                                  row_a[0:1, rcol + 1:rcol + 3], start=True, stop=True),
               wait=[("VE", ve_thr)], inc=["PE"])
        p.emit("vector",
               lambda E: E.tensor_copy(gtmp[:, thrcol:thrcol + 2],
                                       pD[:, pcol + 100:pcol + 102]),
               wait=[("PE", p.cnt("PE"))], inc=["VE"])

    def quant_pass2(p, wname, wpc, eng, qb, qsb, qsb_w, qhbm, thrcol, dst_col,
                    n_row_tiles, row_cols, first_ld_w, qst):
        utmp, qnat = qb
        """Re-stream W, quantize against thresholds, transpose to qsb +
        store to qhbm.  Returns per-piece emitters for interleaving."""
        h = wins[wname]
        tpp, ncc, npieces = q_geom(n_row_tiles, row_cols)
        ldsem = "LD" if eng == "sync" else "LD2"
        ve_thrb = [None]
        qdone_mark = {}
        tr_pe_mark = {}

        def emit_piece(pi):
            if ve_thrb[0] is None:
                ve_thrb[0] = p.cnt("VE")
            b = pi % 2
            w = ([("VE", qdone_mark[(pi - 2, 1)])] if pi >= 2 else list(first_ld_w))
            p.emit(eng,
                   lambda E, pi=pi, b=b: E.dma_start(
                       out=wpc[b][:, :], in_=q_piece_ap(h, row_cols, tpp, ncc, pi)),
                   wait=w, inc=[(ldsem, 16)])
            ld = p.cnt(ldsem)
            blocks = q_piece_blocks(row_cols, tpp, ncc, pi)
            for si in range(2):
                wq = [(ldsem, ld), ("VE", ve_thrb[0]),
                      ("VE", qdone_mark.get((pi - 2, si), 0)),
                      ("VE", qdone_mark.get((pi, si - 1),
                                            qdone_mark.get((pi - 1, 1), qst[0])))]
                wq.append(("PE", tr_pe_mark.get((pi, si - 1),
                                                tr_pe_mark.get((pi - 1, 1), qst[1]))))
                p.emit("vector",
                       lambda E, b=b, si=si: E.tensor_single_scalar(
                           utmp[:, :], wpc[b][:, si * 1024:(si + 1) * 1024],
                           gtmp[:, thrcol + 1:thrcol + 2], ALU.is_lt),
                       wait=wq, inc=["VE"])
                p.emit("vector",
                       lambda E, b=b, si=si: E.scalar_tensor_tensor(
                           qnat[:, :], wpc[b][:, si * 1024:(si + 1) * 1024],
                           gtmp[:, thrcol:thrcol + 1], utmp[:, :],
                           ALU.is_gt, ALU.subtract),
                       wait=[("VE", p.cnt("VE"))], inc=["VE"])
                qdone_mark[(pi, si)] = p.cnt("VE")
                ve_q = p.cnt("VE")
                sblocks = [blk for blk in blocks
                           if si * 1024 <= blk[2] < (si + 1) * 1024]
                for g0 in range(0, len(sblocks), 4):
                    grp = sblocks[g0:g0 + 4]
                    reg = trreg[0]
                    trreg[0] = (trreg[0] + 1) % 4
                    pt = pT16ab[reg]
                    pe_w = [("VE", ve_q), ("SC", trgrp_mark[reg])]
                    for gi, (ti, sub, qcol) in enumerate(grp):
                        p.emit("tensor",
                               lambda E, qcol=qcol, gi=gi, si=si, pt=pt:
                               E.transpose(pt[:, gi * 128:(gi + 1) * 128],
                                           qnat[:, qcol - si * 1024:qcol - si * 1024 + 128],
                                           seye16[:, :]),
                               wait=pe_w if gi == 0 else [], inc=["PE"])
                    pe_t = p.cnt("PE")
                    d0 = dst_col(grp[0][0], grp[0][1])
                    d1 = dst_col(grp[1][0], grp[1][1])
                    p.emit("scalar",
                           lambda E, d0=d0, st=d1 - d0, pt=pt: E.copy(
                               AP(qsb, d0, [[qsb_w, 128], [st, 4], [1, 128]]),
                               pt[:, 0:512]),
                           wait=[("PE", pe_t)], inc=["SC"])
                    trgrp_mark[reg] = p.cnt("SC")
                tr_pe_mark[(pi, si)] = p.cnt("PE")

        def emit_store():
            p.emit("sync", lambda E: E.dma_start(out=qhbm.ap(), in_=qsb[:, :]),
                   wait=[("SC", p.cnt("SC"))], inc=[("ST", 16)])

        return emit_piece, emit_store, npieces

    stk1 = ExitStack()
    _w1q = stk1.enter_context(nc.sbuf_tensor("w1q", [128, NDP * DFF], BF16))
    _w2q = stk1.enter_context(nc.sbuf_tensor("w2q", [128, NFT * D], BF16))
    stk1b = ExitStack()
    gtmp = stk1b.enter_context(nc.sbuf_tensor("gtmp", [128, 40], F32))
    wp0 = stk1b.enter_context(nc.sbuf_tensor("wp0", [128, 2048], F32))
    wp1 = stk1b.enter_context(nc.sbuf_tensor("wp1", [128, 2048], F32))
    wp2 = stk1b.enter_context(nc.sbuf_tensor("wp2", [128, 2048], F32))
    wp3 = stk1b.enter_context(nc.sbuf_tensor("wp3", [128, 2048], F32))
    utmp = stk1b.enter_context(nc.sbuf_tensor("utmp", [128, 1024], F32))
    qnat = stk1b.enter_context(nc.sbuf_tensor("qnat", [128, 1024], BF16))
    utmp2 = stk1b.enter_context(nc.sbuf_tensor("utmp2", [128, 1024], F32))
    qnat2 = stk1b.enter_context(nc.sbuf_tensor("qnat2", [128, 1024], BF16))
    pT16ab = [pC[:, 0:256].bitcast(BF16), pD[:, 0:256].bitcast(BF16),
              pC[:, 256:512].bitcast(BF16), pD[:, 256:512].bitcast(BF16)]
    if True:
        w1q_sb[0] = _w1q
        w2q_sb[0] = _w2q
        trgrp_mark = [0, 0, 0, 0]
        trreg = [0]
        qst = [0, 0]
        p = plan()
        g2 = quant_pass1(p, "w2", [wp2, wp3], "gpsimd", 16, NDP, DFF)
        g1 = quant_pass1(p, "w1", [wp0, wp1], "sync", 0, NFT, D)
        quant_gamma(p, 0, 16, NFT * 128 * D, 0, 0, 36)
        quant_gamma(p, 16, 16, NDP * 128 * DFF, 4, 16, 38)
        em1, st1, np1 = quant_pass2(p, "w1", [wp0, wp1], "sync", (utmp, qnat),
                                    _w1q, NDP * DFF, w1q_h, 36,
                                    lambda ti, sub: sub * DFF + ti * 128, NFT, D,
                                    [("VE", g1[14]), ("VE", g1[15])], qst)
        em2, st2, np2 = quant_pass2(p, "w2", [wp2, wp3], "gpsimd", (utmp2, qnat2),
                                    _w2q, NFT * D, w2q_h, 38,
                                    lambda ti, sub: sub * D + ti * 128, NDP, DFF,
                                    [("VE", g2[14]), ("VE", g2[15])], qst)
        for pi in range(max(np1, np2)):
            if pi < np1:
                em1(pi)
            if pi < np2:
                em2(pi)
        st1()
        st2()
        wq_st_mark[0] = p.cnt("ST")
        p.run("p1")
        stk1b.close()


        def ffn_phase(p, src_kind, dst_kind, bufs, wq_h=None, wq_d=None,
                      first_ve=0, first_sc=0):
            HSC = SCH // 2
            xnat = bufs.get("xnat")
            xtb = bufs["xtb"]
            xb, xsq, yb, ht = bufs["xb"], bufs["xsq"], bufs["yb"], bufs["ht"]
            tmp512, etmp, onat = bufs["tmp512"], bufs["etmp"], bufs.get("onat")
            st_marks = {}
            trev_last = [first_sc, first_sc]
            xb_pe_free = [0, 0]
            xsq_pe_free = [0, 0]
            xtb_free = {}
            onat_free = [("OUT", 0), ("OUT", 0)]
            onat_ev = [0]
            xnat_free = {}
            xnld = {}
            ld_x2 = {}
            xt_wait_map = {}
            ve_y_map = {}
            ve_res_map = {}
            chain_ve = {}
            trev_map = {}
            pF_free = [first_ve]
            pEs_free = [0]
            exp_pe = [0]
            sc_dacc_done = [first_sc]
            sc_h_map = {}
            dacc_sc = {}
            dacc_ve = {}
            relu_marks = {}
            h_bank_free = [0, 0]
            o_bank_free = [0, 0]
            eb_free = [0, 0]

            def emit_xnload(hkey):
                c2, hf2 = divmod(hkey, 2)
                w = [("PE", xnat_free[hkey - 1])] if hkey >= 1 else []
                p.emit("sync",
                       lambda E, c2=c2, hf2=hf2: E.dma_start(
                           out=xnat[0][:, :],
                           in_=AP(x_in, (c2 * SCH + hf2 * 256) * D,
                                  [[D, 128], [128 * D, 2], [1, D]])),
                       wait=w, inc=[("LD", 16)])
                xnld[hkey] = p.cnt("LD")

            st_at_entry = p.cnt("ST")

            def emit_x2load(c):
                cb = c % 2
                w = [xtb_free[c - 2]] if c >= 2 else [("ST", st_at_entry)]
                p.emit("sync",
                       lambda E, c=c, cb=cb: E.dma_start(
                           out=xtb[cb][:, :],
                           in_=AP(x2t_h, c * SCH, [[NDP * S, 128], [S, NDP], [1, SCH]])),
                       wait=w, inc=[("LD", 16)])
                ld_x2[c] = p.cnt("LD")

            def emit_trans(c):
                cb = c % 2
                for hf in range(2):
                    hkey = 2 * c + hf
                    ld_h = xnld[hkey]
                    for dp in range(NDP):
                        pp = pC if dp % 2 == 0 else pD
                        wt = [("LD", ld_h), ("SC", sc_dacc_done[0]),
                              ("SC", trev_last[dp % 2]), ("VE", first_ve)]
                        for st_ in range(2):
                            p.emit("tensor",
                                   lambda E, st_=st_, dp=dp, pp=pp, cb=cb:
                                   E.transpose(pp[:, st_ * 128:(st_ + 1) * 128],
                                               xnat[0][:, st_ * D + dp * 128:
                                                       st_ * D + dp * 128 + 128],
                                               seye32[:, :]),
                                   wait=wt if st_ == 0 else [], inc=["PE"])
                        pe_t = p.cnt("PE")
                        wev = [("PE", pe_t)]
                        if dp == 0 and hf == 0 and c >= 2:
                            wev.append(xtb_free[c - 2])
                        p.emit("scalar",
                               lambda E, dp=dp, hf=hf, pp=pp, cb=cb:
                               E.copy(xtb[cb][:, dp * SCH + hf * 256:
                                              dp * SCH + hf * 256 + 256],
                                      pp[:, 0:256]),
                               wait=wev, inc=["SC"])
                        trev_last[dp % 2] = p.cnt("SC")
                    xnat_free[hkey] = p.cnt("PE")
                    if hkey + 1 < 2 * NCH:
                        emit_xnload(hkey + 1)
                xt_wait_map[c] = [("SC", p.cnt("SC"))]
                trev_map[c] = p.cnt("SC")

            def emit_stats(c):
                cb = c % 2
                xt_wait = (xt_wait_map[c] if src_kind == "nat"
                           else [("LD", ld_x2[c] if c >= 2 else ld_x2[1])])
                for dp in range(NDP):
                    sbb = xb[dp % 2]
                    wcp = xt_wait + [("PE", xb_pe_free[dp % 2])]
                    p.emit("vector",
                           lambda E, cb=cb, dp=dp, sbb=sbb:
                           E.tensor_copy(sbb[:, :], xtb[cb][:, dp * SCH:(dp + 1) * SCH]),
                           wait=wcp, inc=["VE"])
                    wmm = [("VE", p.cnt("VE"))]
                    if dp == 0:
                        wmm.append(("VE", pEs_free[0]))
                    p.emit("tensor",
                           lambda E, dp=dp, sbb=sbb: E.matmul(pE[0:1, 0:512], ones16c[:, :],
                                                              sbb[:, :],
                                                              start=(dp == 0),
                                                              stop=(dp == NDP - 1)),
                           wait=wmm, inc=["PE"])
                    xb_pe_free[dp % 2] = p.cnt("PE")
                for dp in range(NDP):
                    sqq = xsq[dp % 2]
                    wcp = xt_wait + [("PE", xsq_pe_free[dp % 2])]
                    p.emit("scalar",
                           lambda E, cb=cb, dp=dp, sqq=sqq:
                           E.activation(sqq[:, :], xtb[cb][:, dp * SCH:(dp + 1) * SCH],
                                        AF.Square),
                           wait=wcp, inc=["SC"])
                    p.emit("tensor",
                           lambda E, dp=dp, sqq=sqq: E.matmul(pE[0:1, 512:1024], ones16c[:, :],
                                                              sqq[:, :],
                                                              start=(dp == 0),
                                                              stop=(dp == NDP - 1)),
                           wait=[("SC", p.cnt("SC"))], inc=["PE"])
                    xsq_pe_free[dp % 2] = p.cnt("PE")

            def emit_chain(c):
                pe_stats = p.cnt("PE")
                p.emit("vector",
                       lambda E: E.tensor_scalar_mul(row_a[0:1, 0:512], pE[0:1, 0:512], 1.0 / D),
                       wait=[("PE", pe_stats), ("PE", exp_pe[0])], inc=["VE"])
                p.emit("vector",
                       lambda E: E.tensor_mul(row_b[0:1, 0:512], row_a[0:1, 0:512],
                                              row_a[0:1, 0:512]),
                       wait=[("VE", p.cnt("VE"))], inc=["VE"])
                p.emit("vector",
                       lambda E: E.scalar_tensor_tensor(row_c[0:1, 0:512], pE[0:1, 512:1024],
                                                        1.0 / D, row_b[0:1, 0:512],
                                                        ALU.mult, ALU.subtract),
                       wait=[("VE", p.cnt("VE"))], inc=["VE"])
                pEs_free[0] = p.cnt("VE")
                p.emit("scalar",
                       lambda E: E.activation(row_b[0:1, 0:512], row_c[0:1, 0:512], AF.Sqrt,
                                              bias=epst[:, :]),
                       wait=[("VE", p.cnt("VE"))], inc=["SC"])
                p.emit("vector", lambda E: E.reciprocal(row_c[0:1, 0:512], row_b[0:1, 0:512]),
                       wait=[("SC", p.cnt("SC"))], inc=["VE"])
                chain_ve[c] = p.cnt("VE")

            def emit_exp_y(c):
                cb = c % 2
                p.emit("tensor",
                       lambda E: E.matmul(pF[:, 0:512], ones32r[0:1, :], row_a[0:1, 0:512],
                                          start=True, stop=True),
                       wait=[("VE", chain_ve[c]), ("VE", pF_free[0])], inc=["PE"])
                p.emit("tensor",
                       lambda E: E.matmul(pF[:, 512:1024], ones65r[64:65, :], row_c[0:1, 0:512],
                                          start=True, stop=True),
                       wait=[("VE", chain_ve[c])], inc=["PE"])
                pe_exp = p.cnt("PE")
                exp_pe[0] = pe_exp
                for dp in range(NDP):
                    p.emit("vector",
                           lambda E, dp=dp, cb=cb: E.tensor_sub(tmp512[:, :],
                                                                xtb[cb][:, dp * SCH:(dp + 1) * SCH],
                                                                pF[:, 0:512]),
                           wait=[("PE", pe_exp)] if dp == 0 else [], inc=["VE"])
                    p.emit("vector",
                           lambda E, dp=dp: E.tensor_mul(yb[:, dp * SCH:(dp + 1) * SCH],
                                                         tmp512[:, :], pF[:, 512:1024]),
                           inc=["VE"])
                pF_free[0] = p.cnt("VE")
                ve_y_map[c] = p.cnt("VE")

            def emit_h(c, half):
                ve_y = ve_y_map[c]
                for ft in range(NFT):
                    pp = pA if ft % 2 == 0 else pB
                    wt = [("VE", ve_y), ("SC", h_bank_free[ft % 2]),
                          ("VE", onat_ev[0])]
                    if c == 0 and wq_h is not None:
                        wt.append(wq_h[ft])
                    for dp in range(NDP):
                        p.emit("tensor",
                               lambda E, ft=ft, dp=dp, pp=pp, half=half:
                               E.matmul(pp[:, 0:HSC],
                                        w1q_sb[0][:, dp * DFF + ft * 128:dp * DFF + ft * 128 + 128],
                                        yb[:, dp * SCH + half * HSC:dp * SCH + half * HSC + HSC],
                                        start=(dp == 0), stop=(dp == NDP - 1)),
                               wait=wt if dp == 0 else [], inc=["PE"])
                    pe_h = p.cnt("PE")
                    p.emit("scalar",
                           lambda E, ft=ft, pp=pp: E.activation(ht[:, ft * HSC:(ft + 1) * HSC],
                                                                pp[:, 0:HSC], AF.Relu,
                                                                bias=b1T[:, ft:ft + 1]),
                           wait=[("PE", pe_h)], inc=["SC"])
                    h_bank_free[ft % 2] = p.cnt("SC")
                sc_h_map[(c, half)] = p.cnt("SC")

            def emit_out(c, half):
                cb = c % 2
                sc_h = sc_h_map[(c, half)]
                for dp in range(NDP):
                    pp = pC if dp % 2 == 0 else pD
                    wt = [("SC", sc_h), ("SC", o_bank_free[dp % 2])]
                    if c == 0 and wq_d is not None:
                        wt.append(wq_d[dp])
                    if src_kind == "nat" and dp < 2 and c + 1 in trev_map:
                        wt.append(("SC", trev_map[c + 1]))
                    for fp in range(NFT):
                        p.emit("tensor",
                               lambda E, dp=dp, fp=fp, pp=pp:
                               E.matmul(pp[:, 0:HSC],
                                        w2q_sb[0][:, fp * D + dp * 128:fp * D + dp * 128 + 128],
                                        ht[:, fp * HSC:(fp + 1) * HSC],
                                        start=(fp == 0), stop=(fp == NFT - 1)),
                               wait=wt if fp == 0 else [], inc=["PE"])
                    pe_d = p.cnt("PE")
                    eb = etmp[dp % 2]
                    wev = [("PE", pe_d), ("VE", eb_free[dp % 2])]
                    p.emit("scalar",
                           lambda E, dp=dp, pp=pp, eb=eb:
                           E.activation(eb[:, 0:HSC], pp[:, 0:HSC], AF.Identity, scale=0.5,
                                        bias=b2hT[:, dp:dp + 1]),
                           wait=wev, inc=["SC"])
                    o_bank_free[dp % 2] = p.cnt("SC")
                    sc_e = p.cnt("SC")
                    p.emit("vector",
                           lambda E, dp=dp, cb=cb, eb=eb, half=half:
                           E.tensor_add(xtb[cb][:, dp * SCH + half * HSC:
                                                dp * SCH + half * HSC + HSC],
                                        xtb[cb][:, dp * SCH + half * HSC:
                                                dp * SCH + half * HSC + HSC],
                                        eb[:, 0:HSC]),
                           wait=[("SC", sc_e)], inc=["VE"])
                    eb_free[dp % 2] = p.cnt("VE")
                if half == 1:
                    sc_dacc_done[0] = p.cnt("SC")
                    ve_res_map[c] = p.cnt("VE")

            def emit_store(c):
                cb = c % 2
                p.emit("scalar",
                       lambda E, c=c, cb=cb: E.dma_start(
                           out=AP(x1t_h, c * SCH, [[NDP * S, 128], [S, NDP], [1, SCH]]),
                           in_=xtb[cb][:, :]),
                       wait=[("VE", ve_res_map[c])], inc=[("ST", 16)])
                st_marks[c] = p.cnt("ST")
                xtb_free[c] = ("ST", p.cnt("ST"))
                if "st_out" in bufs:
                    bufs["st_out"][c] = p.cnt("ST")

            def emit_outstore(c):
                cb = c % 2
                for sg in range(4):
                    ob = sg % 2
                    for half in range(2):
                        php = pA if half == 0 else pB
                        wt = [("VE", ve_res_map[c]), ("VE", onat_ev[0]),
                              ("SC", h_bank_free[half])]
                        wt.append(onat_free[ob])
                        for j in range(4):
                            dp = half * 4 + j
                            p.emit("tensor",
                                   lambda E, sg=sg, dp=dp, j=j, php=php, cb=cb:
                                   E.transpose(php[:, j * 128:(j + 1) * 128],
                                               xtb[cb][:, dp * SCH + sg * 128:
                                                       dp * SCH + sg * 128 + 128],
                                               seye32[:, :]),
                                   wait=wt if j == 0 else [], inc=["PE"])
                        pe_t = p.cnt("PE")
                        p.emit("vector",
                               lambda E, ob=ob, half=half, php=php:
                               E.tensor_copy(onat[:, ob * D + half * 512:
                                                  ob * D + half * 512 + 512],
                                             php[:, :]),
                               wait=[("PE", pe_t)], inc=["VE"])
                        onat_ev[0] = p.cnt("VE")
                    p.emit("sync",
                           lambda E, c=c, sg=sg, ob=ob: E.dma_start(
                               out=AP(out_t, (c * SCH + sg * 128) * D,
                                      [[D, 128], [1, D]]),
                               in_=onat[:, ob * D:(ob + 1) * D]),
                           wait=[("VE", onat_ev[0])], inc=[("OUT", 16)])
                    onat_free[ob] = ("OUT", p.cnt("OUT"))
                xtb_free[c] = ("PE", p.cnt("PE"))
                st_marks[c] = p.cnt("ST")

            if src_kind == "nat":
                emit_xnload(0)
                emit_trans(0)
            else:
                emit_x2load(0)
                emit_x2load(1)
                pass
            emit_stats(0)
            emit_chain(0)
            emit_exp_y(0)
            pend_outstore = [None]
            PIPE = True
            if PIPE:
                for c in range(NCH):
                    emit_h(c, 0)
                    if c + 1 < NCH:
                        if src_kind == "nat":
                            emit_trans(c + 1)
                        emit_stats(c + 1)
                        emit_chain(c + 1)
                    emit_out(c, 0)
                    emit_h(c, 1)
                    if c + 1 < NCH:
                        emit_exp_y(c + 1)
                    emit_out(c, 1)
                    if dst_kind == "x1t":
                        emit_store(c)
                    else:
                        emit_outstore(c)
                        if 2 <= c + 2 < NCH:
                            emit_x2load(c + 2)
            else:
                for c in range(NCH):
                    emit_h(c, 0)
                    emit_out(c, 0)
                    emit_h(c, 1)
                    emit_out(c, 1)
                    if dst_kind == "x1t":
                        emit_store(c)
                    else:
                        emit_outstore(c)
                    if c + 1 < NCH:
                        if src_kind == "nat":
                            emit_trans(c + 1)
                        emit_stats(c + 1)
                        emit_chain(c + 1)
                        emit_exp_y(c + 1)
                        if bufs.get("dbg_x2chk") and c + 1 == 1:
                            pass
                    if src_kind != "nat" and 2 <= c + 2 < NCH:
                        emit_x2load(c + 2)
            p.emit("gpsimd", lambda E: E.nop(),
                   wait=[("ST", p.cnt("ST")), ("OUT", p.cnt("OUT"))])

        stk2 = ExitStack()
        xnat0 = stk2.enter_context(nc.sbuf_tensor("xnat0", [128, 2 * D], F32))
        f_xt0 = stk2.enter_context(nc.sbuf_tensor("f_xt0", [128, NDP * SCH], F32))
        f_xt1 = stk2.enter_context(nc.sbuf_tensor("f_xt1", [128, NDP * SCH], F32))
        f_xb0 = stk2.enter_context(nc.sbuf_tensor("f_xb0", [128, SCH], BF16))
        f_xb1 = stk2.enter_context(nc.sbuf_tensor("f_xb1", [128, SCH], BF16))
        f_xq0 = stk2.enter_context(nc.sbuf_tensor("f_xq0", [128, SCH], BF16))
        f_xq1 = stk2.enter_context(nc.sbuf_tensor("f_xq1", [128, SCH], BF16))
        f_yb = stk2.enter_context(nc.sbuf_tensor("f_yb", [128, NDP * SCH], BF16))
        f_ht = stk2.enter_context(nc.sbuf_tensor("f_ht", [128, NFT * (SCH // 2)], BF16))
        f_et0 = stk2.enter_context(nc.sbuf_tensor("f_et0", [128, 512], F32))
        f_et1 = stk2.enter_context(nc.sbuf_tensor("f_et1", [128, 512], F32))
        f_tmp = stk2.enter_context(nc.sbuf_tensor("f_tmp", [128, 512], F32))
        if True:
            p = plan()
            trev_marks = {}
            dacc_sc = {}
            bufs = dict(xnat=[xnat0], xtb=[f_xt0, f_xt1], xb=[f_xb0, f_xb1],
                        xsq=[f_xq0, f_xq1], yb=f_yb, ht=f_ht, tmp512=f_tmp,
                        etmp=[f_et0, f_et1], st_out=ffn1_st)
            sc_q = p.cnt("SC")
            ffn_phase(p, "nat", "x1t", bufs,
                      [("SC", sc_q)] * NFT, [("SC", sc_q)] * NDP,
                      first_ve=p.cnt("VE"), first_sc=sc_q)
            p.run("ffn1")
            stk2.close()
        stk1.close()

    if debug_stage == "x1":
        return _finish_debug(nc, plan, counts, out_t, x1t_h, F32, NDP * S)
    if debug_stage in ("w1q", "w2q"):
        return _finish_debug(nc, plan, counts, out_t,
                             w1q_h if debug_stage == "w1q" else w2q_h, BF16,
                             NDP * DFF if debug_stage == "w1q" else NFT * D)

    stk3 = ExitStack()
    wdq16 = stk3.enter_context(nc.sbuf_tensor("wdq16", [128, NDP * DQ], BF16))
    wdkv16 = stk3.enter_context(nc.sbuf_tensor("wdkv16", [128, NDP * DKV], BF16))
    wuq16 = stk3.enter_context(nc.sbuf_tensor("wuq16", [128, NQP * D], BF16))
    wqr16 = stk3.enter_context(nc.sbuf_tensor("wqr16", [128, NQP * D], BF16))
    wuk16 = stk3.enter_context(nc.sbuf_tensor("wuk16", [128, NKP * D], BF16))
    wkr16 = stk3.enter_context(nc.sbuf_tensor("wkr16", [128, NDP * D], BF16))
    wuv16 = stk3.enter_context(nc.sbuf_tensor("wuv16", [128, NKP * D], BF16))
    q_x0 = stk3.enter_context(nc.sbuf_tensor("q_x0", [128, NDP * SCH], F32))
    q_x1 = stk3.enter_context(nc.sbuf_tensor("q_x1", [128, NDP * SCH], F32))
    q_xb = stk3.enter_context(nc.sbuf_tensor("q_xb", [128, NDP * SCH], BF16))
    q_xsq = stk3.enter_context(nc.sbuf_tensor("q_xsq", [128, NDP * SCH], BF16))
    q_y10 = stk3.enter_context(nc.sbuf_tensor("q_y10", [128, NDP * SCH], BF16))
    q_y11 = stk3.enter_context(nc.sbuf_tensor("q_y11", [128, NDP * SCH], BF16))
    q_cq = stk3.enter_context(nc.sbuf_tensor("q_cq", [128, NQP * SCH], BF16))
    q_ckv = stk3.enter_context(nc.sbuf_tensor("q_ckv", [128, NKP * SCH], BF16))
    q_rs = stk3.enter_context(nc.sbuf_tensor("q_rs", [128, SCH], BF16))
    k_rs = stk3.enter_context(nc.sbuf_tensor("k_rs", [128, SCH], BF16))
    qsp0 = stk3.enter_context(nc.sbuf_tensor("qsp0", [128, SCH], BF16))
    qsp1 = stk3.enter_context(nc.sbuf_tensor("qsp1", [128, SCH], BF16))
    ksp0 = stk3.enter_context(nc.sbuf_tensor("ksp0", [128, SCH], BF16))
    ksp1 = stk3.enter_context(nc.sbuf_tensor("ksp1", [128, SCH], BF16))
    vga = stk3.enter_context(nc.sbuf_tensor("vga", [128, NDP * 520], BF16))
    vgb = stk3.enter_context(nc.sbuf_tensor("vgb", [128, NDP * 520], BF16))
    bvb = stk3.enter_context(nc.sbuf_tensor("bvb", [128, D], F32))
    q_tmp = stk3.enter_context(nc.sbuf_tensor("q_tmp", [128, 512], F32))
    Ctab = stk3.enter_context(nc.sbuf_tensor("Ctab", [32, SCH], BF16))
    Stab = stk3.enter_context(nc.sbuf_tensor("Stab", [32, SCH], BF16))
    msb = stk3.enter_context(nc.sbuf_tensor("msb", [32, SCH], F32))
    rrsb = stk3.enter_context(nc.sbuf_tensor("rrsb", [32, SCH], F32))
    fx1 = stk3.enter_context(nc.sbuf_tensor("fx1", [32, SCH], F32))
    fx2 = stk3.enter_context(nc.sbuf_tensor("fx2", [32, SCH], F32))
    fx3 = stk3.enter_context(nc.sbuf_tensor("fx3", [32, SCH], F32))
    swsb = stk3.enter_context(nc.sbuf_tensor("swsb", [32, SCH], BF16))
    posrow = stk3.enter_context(nc.sbuf_tensor("posrow", [1, SCH], F32))
    bvr32 = stk3.enter_context(nc.sbuf_tensor("bvr32", [1, D], F32))
    if True:
        p = plan()
        for dst, nm, rows, cols, nt in [
            (wdq16, "wdq", D, DQ, NDP), (wdkv16, "wdkv", D, DKV, NDP),
            (wuq16, "wuq", DQ, D, NQP), (wqr16, "wqr", DQ, D, NQP),
            (wuk16, "wuk", DKV, D, NKP), (wkr16, "wkr", D, D, NDP),
            (wuv16, "wuv", DKV, D, NKP),
        ]:
            p.emit("gpsimd",
                   lambda E, dst=dst, nm=nm, cols=cols, nt=nt:
                   E.dma_start(out=dst[:, :],
                               in_=AP(wins[nm], 0, [[cols, 128], [128 * cols, nt], [1, cols]])),
                   inc=[("LD", 16)])
        p.emit("sync", lambda E: E.dma_start(out=bvr32[0:1, :],
                                             in_=wins["buv"].ap().unsqueeze(0)),
               inc=[("LD", 16)])
        ld_w = p.cnt("LD")
        for vst in (vga, vgb):
            for off in (64, 129):
                p.emit("vector",
                       lambda E, vst=vst, off=off: E.memset(
                           AP(vst, off, [[NDP * 520, 128], [520, 8], [130, 4], [1, 1]]), 1.0),
                       inc=["VE"])
        p.emit("tensor",
               lambda E: E.matmul(pF[:, 0:512], ones32r[0:1, :],
                                  bvr32[0:1, 0:512], start=True, stop=True),
               wait=[("LD", ld_w), ("VE", p.cnt("VE")), ("SC", p.cnt("SC"))],
               inc=["PE"])
        p.emit("tensor",
               lambda E: E.matmul(pF[:, 512:1024], ones32r[0:1, :],
                                  bvr32[0:1, 512:1024], start=True, stop=True),
               inc=["PE"])
        pe_bvb = p.cnt("PE")
        p.emit("vector", lambda E: E.tensor_copy(bvb[:, 0:512], pF[:, 0:512]),
               wait=[("PE", pe_bvb)], inc=["VE"])
        p.emit("vector", lambda E: E.tensor_copy(bvb[:, 512:1024], pF[:, 512:1024]),
               inc=["VE"])
        gp_bvb = 0
        bvb_ve = p.cnt("VE")

        x1c = [q_x0, q_x1]
        x1c_free = {}
        pe_stats_prev = [0]
        pEs_free = [0]
        pF_free = [bvb_ve]
        tabfree_ve = [0]
        ropefree_sc = [0]
        qsp_st = {}
        ksp_st = {}
        vst_st = {}
        v_evac_last = [0]
        ct_ev_prev = [0, 0]
        pe_posrow_used = [0]
        qr_sc_free = [0]
        kr_sc_free = [0]
        combine_ve = [0, 0]
        ld_x1 = {}

        def emit_x1load(c):
            cb = c % 2
            w = [("ST", ffn1_st[c])]
            if c >= 2:
                w.append(("VE", x1c_free[c - 2]))
            p.emit("sync",
                   lambda E, c=c, cb=cb: E.dma_start(
                       out=x1c[cb][:, :],
                       in_=AP(x1t_h, c * SCH, [[NDP * S, 128], [S, NDP], [1, SCH]])),
                   wait=w, inc=[("LD", 16)])
            ld_x1[c] = p.cnt("LD")

        emit_x1load(0)
        emit_x1load(1)
        q_y1b = [q_y10, q_y11]
        ve_y_q = {}
        sc_ct_q = {}
        sc_tab_q = {}
        ln_chain_ve = {}

        def emit_tables(c):
            p.emit("gpsimd",
                   lambda E, c=c: E.iota(posrow[0:1, :], [[1, SCH]], base=c * SCH,
                                         channel_multiplier=0,
                                         allow_small_or_imprecise_dtypes=True),
                   wait=[("PE", pe_posrow_used[0])], inc=["GP"])
            p.emit("tensor",
                   lambda E, c=c: E.matmul(pE[0:32, 0:512], thdup[0:1, :],
                                           posrow[0:1, :],
                                           start=True, stop=True),
                   wait=[("VE", tabfree_ve[0]), ("SC", ropefree_sc[0]),
                         ("GP", p.cnt("GP"))], inc=["PE"])
            pe_posrow_used[0] = p.cnt("PE")
            pe_r = p.cnt("PE")
            p.emit("vector", lambda E: E.tensor_scalar(msb[:, :], pE[0:32, 0:512], PI, -2.0 * PI,
                                                       ALU.is_ge, ALU.mult),
                   wait=[("PE", pe_r)], inc=["VE"])
            p.emit("vector", lambda E: E.tensor_add(rrsb[:, :], msb[:, :], pE[0:32, 0:512]),
                   wait=[("VE", p.cnt("VE"))], inc=["VE"])
            p.emit("scalar", lambda E: E.activation(Stab[:, :], rrsb[:, :], AF.Sin),
                   wait=[("VE", p.cnt("VE"))], inc=["SC"])
            sc_s = p.cnt("SC")
            p.emit("vector", lambda E: E.tensor_scalar(msb[:, :], pE[0:32, 0:512], 0.5 * PI,
                                                       -2.0 * PI, ALU.is_ge, ALU.mult),
                   wait=[("SC", sc_s)], inc=["VE"])
            p.emit("vector", lambda E: E.tensor_add(rrsb[:, :], msb[:, :], pE[0:32, 0:512]),
                   wait=[("VE", p.cnt("VE"))], inc=["VE"])
            tabfree_ve[0] = p.cnt("VE")
            tab_ve = p.cnt("VE")
            p.emit("scalar", lambda E: E.activation(Ctab[:, :], rrsb[:, :], AF.Sin,
                                                    bias=halfpi32[:, :]),
                   wait=[("VE", tab_ve)], inc=["SC"])
            sc_tab_q[c] = p.cnt("SC")

        ln_cp = {}

        def emit_ln_copies(c):
            cb = c % 2
            xt_wait = [("LD", ld_x1[c] if c >= 2 else ld_x1[1])]
            p.emit("vector", lambda E, cb=cb: E.tensor_copy(q_xb[:, :], x1c[cb][:, :]),
                   wait=xt_wait + [("PE", pe_stats_prev[0])], inc=["VE"])
            p.emit("scalar", lambda E, cb=cb: E.activation(q_xsq[:, :], x1c[cb][:, :], AF.Square),
                   wait=xt_wait + [("PE", pe_stats_prev[0])], inc=["SC"])
            ln_cp[c] = (p.cnt("VE"), p.cnt("SC"))

        def emit_ln(c):
            cb = c % 2
            y1 = q_y1b[cb]
            ve_xb, sc_sq = ln_cp[c]
            for dp in range(NDP):
                p.emit("tensor",
                       lambda E, dp=dp: E.matmul(pE[0:1, 512:1024], ones16c[:, :],
                                                 q_xb[:, dp * SCH:(dp + 1) * SCH],
                                                 start=(dp == 0), stop=(dp == NDP - 1)),
                       wait=[("VE", ve_xb), ("VE", pEs_free[0])] if dp == 0 else [], inc=["PE"])
            for dp in range(NDP):
                p.emit("tensor",
                       lambda E, dp=dp: E.matmul(pF[0:1, 0:512], ones16c[:, :],
                                                 q_xsq[:, dp * SCH:(dp + 1) * SCH],
                                                 start=(dp == 0), stop=(dp == NDP - 1)),
                       wait=([("SC", sc_sq)] if dp == 0 else []) +
                            ([("VE", pF_free[0])] if dp == 0 else []),
                       inc=["PE"])
            pe_stats = p.cnt("PE")
            pe_stats_prev[0] = pe_stats
            p.emit("vector",
                   lambda E: E.tensor_scalar_mul(row_a[0:1, 0:512], pE[0:1, 512:1024], 1.0 / D),
                   wait=[("PE", pe_stats)], inc=["VE"])
            pEs_free[0] = p.cnt("VE")
            p.emit("vector", lambda E: E.tensor_mul(row_b[0:1, 0:512], row_a[0:1, 0:512],
                                                    row_a[0:1, 0:512]),
                   wait=[("VE", p.cnt("VE"))], inc=["VE"])
            p.emit("vector",
                   lambda E: E.scalar_tensor_tensor(row_c[0:1, 0:512], pF[0:1, 0:512], 1.0 / D,
                                                    row_b[0:1, 0:512], ALU.mult, ALU.subtract),
                   wait=[("VE", p.cnt("VE"))], inc=["VE"])
            p.emit("scalar", lambda E: E.activation(row_b[0:1, 0:512], row_c[0:1, 0:512], AF.Sqrt,
                                                    bias=epst[:, :]),
                   wait=[("VE", p.cnt("VE"))], inc=["SC"])
            p.emit("vector", lambda E: E.reciprocal(row_c[0:1, 0:512], row_b[0:1, 0:512]),
                   wait=[("SC", p.cnt("SC"))], inc=["VE"])
            ln_chain_ve[c] = p.cnt("VE")

        def emit_ln_y(c):
            cb = c % 2
            y1 = q_y1b[cb]
            ve_st = ln_chain_ve[c]
            p.emit("tensor", lambda E: E.matmul(pF[:, 0:512], ones32r[0:1, :], row_a[0:1, 0:512],
                                                start=True, stop=True),
                   wait=[("VE", ve_st)], inc=["PE"])
            p.emit("tensor", lambda E: E.matmul(pF[:, 512:1024], ones65r[64:65, :], row_c[0:1, 0:512],
                                                start=True, stop=True),
                   wait=[("VE", ve_st)], inc=["PE"])
            pe_exp = p.cnt("PE")
            for dp in range(NDP):
                p.emit("vector",
                       lambda E, dp=dp, cb=cb: E.tensor_sub(q_tmp[:, :],
                                                            x1c[cb][:, dp * SCH:(dp + 1) * SCH],
                                                            pF[:, 0:512]),
                       wait=[("PE", pe_exp)] if dp == 0 else [], inc=["VE"])
                p.emit("vector",
                       lambda E, dp=dp, y1=y1: E.tensor_mul(y1[:, dp * SCH:(dp + 1) * SCH],
                                                            q_tmp[:, :], pF[:, 512:1024]),
                       inc=["VE"])
            pF_free[0] = p.cnt("VE")
            x1c_free[c] = p.cnt("VE")
            ve_y_q[c] = p.cnt("VE")

        def emit_ct(c):
            y1 = q_y1b[c % 2]
            for i, (wsb, bT, dst, nop) in enumerate([(wdq16, bdqT, q_cq, NQP),
                                                     (wdkv16, bdkvT, q_ckv, NKP)]):
                for op in range(nop):
                    key = i * NQP + op
                    pp = pA if key % 2 == 0 else pB
                    wt = [("VE", ve_y_q[c]), ("SC", ct_ev_prev[key % 2])]
                    if c == 0 and key < 2:
                        wt.append(("LD", ld_w))
                    wt.append(("VE", combine_ve[key % 2]))
                    for dp in range(NDP):
                        p.emit("tensor",
                               lambda E, wsb=wsb, op=op, dp=dp, pp=pp, y1=y1:
                               E.matmul(pp[:, :],
                                        wsb[:, dp * 512 + op * 128:dp * 512 + op * 128 + 128],
                                        y1[:, dp * SCH:(dp + 1) * SCH],
                                        start=(dp == 0), stop=(dp == NDP - 1)),
                               wait=wt if dp == 0 else [], inc=["PE"])
                    pe_c = p.cnt("PE")
                    p.emit("scalar",
                           lambda E, dst=dst, bT=bT, op=op, pp=pp:
                           E.activation(dst[:, op * SCH:(op + 1) * SCH], pp[:, :], AF.Identity,
                                        bias=bT[:, op:op + 1]),
                           wait=[("PE", pe_c)], inc=["SC"])
                    ct_ev_prev[key % 2] = p.cnt("SC")
            sc_ct_q[c] = p.cnt("SC")

        def emit_v(c):
            cb = c % 2
            vb = [vga, vgb][cb]
            wvst = [("ST", vst_st[c - 2])] if c >= 2 else []
            v_evac = {}
            for tg in range(2):
                for j in range(4):
                    g = tg * 4 + j
                    reg = g % 2
                    wt = [("SC", sc_ct_q[c]), ("VE", pF_free[0]),
                          ("VE", v_evac.get(g - 2, 0))]
                    if c == 0 and g == 0:
                        wt.append(("LD", ld_w))
                    for kp in range(NKP):
                        p.emit("tensor",
                               lambda E, j=j, kp=kp, tg=tg, reg=reg: E.matmul(
                                   pF[:, reg * 512:(reg + 1) * 512],
                                   q_ckv[:, kp * SCH + j * 128:kp * SCH + j * 128 + 128],
                                   wuv16[:, kp * D + tg * 512:kp * D + tg * 512 + 512],
                                   start=(kp == 0), stop=(kp == NKP - 1)),
                               wait=wt if kp == 0 else [], inc=["PE"])
                    pe_v = p.cnt("PE")
                    wev = [("PE", pe_v)]
                    if g == 0:
                        wev += wvst + [("GP", gp_bvb)]
                    for trel in range(4):
                        p.emit("vector",
                               lambda E, j=j, tg=tg, reg=reg, vb=vb, trel=trel:
                               E.tensor_tensor(
                                   AP(vb, (tg * 4 + trel) * 520 + j * 130,
                                      [[NDP * 520, 128], [65, 2], [1, 64]]),
                                   AP(pF, reg * 512 + trel * 128,
                                      [[1024, 128], [64, 2], [1, 64]]),
                                   AP(bvb, tg * 512 + trel * 128,
                                      [[D, 128], [64, 2], [1, 64]]),
                                   ALU.add),
                               wait=wev if trel == 0 else [], inc=["VE"])
                    v_evac[g] = p.cnt("VE")
            ve_v = p.cnt("VE")
            pF_free[0] = ve_v
            for t8 in range(NDP):
                p.emit("sync",
                       lambda E, c=c, vb=vb, t8=t8: E.dma_start(
                           out=AP(v_h, t8 * 4160 + c * 520, [[NDP * 4160, 128], [1, 520]]),
                           in_=vb[:, t8 * 520:(t8 + 1) * 520]),
                       wait=[("VE", ve_v)] if t8 == 0 else [], inc=[("ST", 16)])
            vst_st[c] = p.cnt("ST")

        def emit_tsec(c, t):
            y1 = q_y1b[c % 2]
            sc_ct = sc_ct_q[c]
            tb = t % 2
            wt = [("SC", sc_ct), ("VE", combine_ve[0])]
            if c == 0 and t == 0:
                wt.append(("LD", ld_w))
            for qp in range(NQP):
                p.emit("tensor",
                       lambda E, qp=qp, t=t: E.matmul(
                           pA[:, :], wuq16[:, qp * D + t * 128:qp * D + t * 128 + 128],
                           q_cq[:, qp * SCH:(qp + 1) * SCH],
                           start=(qp == 0), stop=(qp == NQP - 1)),
                       wait=wt if qp == 0 else [], inc=["PE"])
            pe_qc = p.cnt("PE")
            wt = [("SC", sc_ct), ("SC", qr_sc_free[0])]
            for qp in range(NQP):
                p.emit("tensor",
                       lambda E, qp=qp, t=t: E.matmul(
                           pC[:, :], wqr16[:, qp * D + t * 128:qp * D + t * 128 + 128],
                           q_cq[:, qp * SCH:(qp + 1) * SCH],
                           start=(qp == 0), stop=(qp == NQP - 1)),
                       wait=wt if qp == 0 else [], inc=["PE"])
            pe_qr = p.cnt("PE")
            bias_q = bQcT if t > 0 else bqrT
            p.emit("scalar",
                   lambda E, t=t, bias_q=bias_q: E.activation(q_rs[:, :], pC[:, :], AF.Identity,
                                                              bias=bias_q[:, t:t + 1]),
                   wait=[("PE", pe_qr), ("VE", combine_ve[0])], inc=["SC"])
            sc_qrs = p.cnt("SC")
            qr_sc_free[0] = sc_qrs
            qb = [qsp0, qsp1][tb]
            wspill = ([("ST", qsp_st[t - 2])] if t >= 2 else
                      ([("ST", qsp_st[t + 6])] if (t + 6) in qsp_st else []))
            if t > 0:
                p.emit("vector",
                       lambda E, qb=qb: E.tensor_add(qb[:, :], q_rs[:, :], pA[:, :]),
                       wait=[("SC", sc_qrs), ("PE", pe_qc)] + wspill, inc=["VE"])
            else:
                p.emit("vector",
                       lambda E, qb=qb: E.scalar_tensor_tensor(
                           qb[:, :], pA[:, :], buqT[:, 0:1],
                           q_rs[:, :], ALU.add, ALU.add),
                       wait=[("SC", sc_qrs), ("PE", pe_qc)] + wspill,
                       inc=["VE"])
                p.emit("tensor",
                       lambda E: E.matmul(pE[0:32, 0:512], spsw[:, :], q_rs[0:32, :],
                                          start=True, stop=True),
                       wait=[("SC", sc_qrs), ("VE", tabfree_ve[0])], inc=["PE"])
                pe_sw = p.cnt("PE")
                p.emit("scalar",
                       lambda E: E.activation(swsb[:, :], pE[0:32, 0:512], AF.Identity),
                       wait=[("PE", pe_sw), ("SC", sc_tab_q[c])], inc=["SC"])
                ropefree_sc[0] = p.cnt("SC")
                sc_sw = p.cnt("SC")
                p.emit("vector", lambda E: E.tensor_mul(fx1[:, :], q_rs[0:32, :], Ctab[:, :]),
                       wait=[("SC", sc_sw)], inc=["VE"])
                p.emit("vector", lambda E: E.tensor_mul(fx2[:, :], swsb[:, :], Stab[:, :]),
                       inc=["VE"])
                p.emit("vector",
                       lambda E: E.scalar_tensor_tensor(fx3[:, :], fx1[:, :], buqT[0:32, 0:1],
                                                        fx2[:, :], ALU.add, ALU.add),
                       wait=[("VE", p.cnt("VE"))], inc=["VE"])
                p.emit("vector",
                       lambda E, qb=qb: E.tensor_add(qb[0:32, :], fx3[:, :], pA[0:32, :]),
                       wait=[("VE", p.cnt("VE"))], inc=["VE"])
            combine_ve[0] = p.cnt("VE")
            p.emit("sync",
                   lambda E, t=t, c=c, qb=qb: E.dma_start(
                       out=AP(qt_h, t * S + c * SCH, [[NDP * S, 128], [1, SCH]]),
                       in_=qb[:, :]),
                   wait=[("VE", combine_ve[0])], inc=[("ST", 16)])
            qsp_st[t] = p.cnt("ST")

            wt = [("SC", sc_ct), ("VE", combine_ve[1])]
            for kp in range(NKP):
                p.emit("tensor",
                       lambda E, kp=kp, t=t: E.matmul(
                           pB[:, :], wuk16[:, kp * D + t * 128:kp * D + t * 128 + 128],
                           q_ckv[:, kp * SCH:(kp + 1) * SCH],
                           start=(kp == 0), stop=(kp == NKP - 1)),
                       wait=wt if kp == 0 else [], inc=["PE"])
            pe_kc = p.cnt("PE")
            wt = [("VE", ve_y_q[c]), ("SC", kr_sc_free[0])]
            for dp in range(NDP):
                p.emit("tensor",
                       lambda E, dp=dp, t=t, y1=y1: E.matmul(
                           pD[:, :], wkr16[:, dp * D + t * 128:dp * D + t * 128 + 128],
                           y1[:, dp * SCH:(dp + 1) * SCH],
                           start=(dp == 0), stop=(dp == NDP - 1)),
                       wait=wt if dp == 0 else [], inc=["PE"])
            pe_kr = p.cnt("PE")
            bias_k = bKcT if t > 0 else bkrT
            p.emit("scalar",
                   lambda E, t=t, bias_k=bias_k: E.activation(k_rs[:, :], pD[:, :], AF.Identity,
                                                              bias=bias_k[:, t:t + 1]),
                   wait=[("PE", pe_kr), ("VE", combine_ve[1])], inc=["SC"])
            sc_krs = p.cnt("SC")
            kr_sc_free[0] = sc_krs
            kb = [ksp0, ksp1][tb]
            wspill = ([("ST", ksp_st[t - 2])] if t >= 2 else
                      ([("ST", ksp_st[t + 6])] if (t + 6) in ksp_st else []))
            if t > 0:
                p.emit("vector",
                       lambda E, kb=kb: E.tensor_add(kb[:, :], k_rs[:, :], pB[:, :]),
                       wait=[("SC", sc_krs), ("PE", pe_kc)] + wspill, inc=["VE"])
            else:
                p.emit("vector",
                       lambda E, kb=kb: E.scalar_tensor_tensor(
                           kb[:, :], pB[:, :], bukT[:, 0:1],
                           k_rs[:, :], ALU.add, ALU.add),
                       wait=[("SC", sc_krs), ("PE", pe_kc)] + wspill,
                       inc=["VE"])
                p.emit("tensor",
                       lambda E: E.matmul(pE[0:32, 0:512], spsw[:, :], k_rs[0:32, :],
                                          start=True, stop=True),
                       wait=[("SC", sc_krs), ("SC", ropefree_sc[0])], inc=["PE"])
                pe_sw = p.cnt("PE")
                p.emit("scalar",
                       lambda E: E.activation(swsb[:, :], pE[0:32, 0:512], AF.Identity),
                       wait=[("PE", pe_sw), ("VE", p.cnt("VE"))], inc=["SC"])
                ropefree_sc[0] = p.cnt("SC")
                sc_sw = p.cnt("SC")
                p.emit("vector", lambda E: E.tensor_mul(fx1[:, :], k_rs[0:32, :], Ctab[:, :]),
                       wait=[("SC", sc_sw)], inc=["VE"])
                p.emit("vector", lambda E: E.tensor_mul(fx2[:, :], swsb[:, :], Stab[:, :]),
                       inc=["VE"])
                p.emit("vector",
                       lambda E: E.scalar_tensor_tensor(fx3[:, :], fx1[:, :], bukT[0:32, 0:1],
                                                        fx2[:, :], ALU.add, ALU.add),
                       wait=[("VE", p.cnt("VE"))], inc=["VE"])
                p.emit("vector",
                       lambda E, kb=kb: E.tensor_add(kb[0:32, :], fx3[:, :], pB[0:32, :]),
                       wait=[("VE", p.cnt("VE"))], inc=["VE"])
            combine_ve[1] = p.cnt("VE")
            p.emit("sync",
                   lambda E, t=t, c=c, kb=kb: E.dma_start(
                       out=AP(kt_h, t * S + c * SCH, [[NDP * S, 128], [1, SCH]]),
                       in_=kb[:, :]),
                   wait=[("VE", combine_ve[1])], inc=[("ST", 16)])
            ksp_st[t] = p.cnt("ST")

        emit_tables(0)
        emit_ln_copies(0)
        emit_ln(0)
        emit_ln_y(0)
        for c in range(NCH):
            emit_ct(c)
            emit_v(c)
            for t in range(NDP):
                emit_tsec(c, t)
                if t == 1 and c + 1 < NCH:
                    emit_tables(c + 1)
                    emit_ln_copies(c + 1)
                if t == 2 and c + 1 < NCH:
                    emit_ln(c + 1)
                if t == 4 and c + 1 < NCH:
                    emit_ln_y(c + 1)
                if t == 3 and c + 2 < NCH:
                    emit_x1load(c + 2)
            p.emit("gpsimd", lambda E: E.nop(),
                   wait=[("ST", p.cnt("ST")), ("OUT", p.cnt("OUT"))])
        qkv_ld_gate = {t: max(qsp_st[t], ksp_st[t], vst_st[NCH - 1])
                       for t in range(NDP)}
        p.run("qkv")
        stk3.close()

    if debug_stage in ("qt", "kt", "v"):
        src = {"qt": qt_h, "kt": kt_h, "v": v_h}[debug_stage]
        return _finish_debug(nc, plan, counts, out_t, src, BF16,
                             NDP * S if debug_stage != "v" else NDP * 4160)

    if debug_stage == "x1aq":
        return _finish_debug(nc, plan, counts, out_t, x1t_h, F32, NDP * S)

    stk4 = ExitStack()
    ktt0 = stk4.enter_context(nc.sbuf_tensor("ktt0", [128, S], BF16))
    ktt1 = stk4.enter_context(nc.sbuf_tensor("ktt1", [128, S], BF16))
    qtt0 = stk4.enter_context(nc.sbuf_tensor("qtt0", [128, S], BF16))
    qtt1 = stk4.enter_context(nc.sbuf_tensor("qtt1", [128, S], BF16))
    vt0 = stk4.enter_context(nc.sbuf_tensor("vt0", [128, 4160], BF16))
    vt1 = stk4.enter_context(nc.sbuf_tensor("vt1", [128, 4160], BF16))
    pr0 = stk4.enter_context(nc.sbuf_tensor("pr0", [128, 1024], BF16))
    pr1 = stk4.enter_context(nc.sbuf_tensor("pr1", [128, 1024], BF16))
    pr2 = stk4.enter_context(nc.sbuf_tensor("pr2", [128, 1024], BF16))
    pr3 = stk4.enter_context(nc.sbuf_tensor("pr3", [128, 1024], BF16))
    pr4 = stk4.enter_context(nc.sbuf_tensor("pr4", [128, 1024], BF16))
    pr5 = stk4.enter_context(nc.sbuf_tensor("pr5", [128, 1024], BF16))
    rbx0 = stk4.enter_context(nc.sbuf_tensor("rbx0", [64, 1024], F32))
    rbx1 = stk4.enter_context(nc.sbuf_tensor("rbx1", [64, 1024], F32))
    rcp0 = stk4.enter_context(nc.sbuf_tensor("rcp0", [1, 1024], F32))
    rcp1 = stk4.enter_context(nc.sbuf_tensor("rcp1", [1, 1024], F32))
    avst0 = stk4.enter_context(nc.sbuf_tensor("avst0", [65, 1024], F32))
    avst1 = stk4.enter_context(nc.sbuf_tensor("avst1", [65, 1024], F32))
    qz0 = stk4.enter_context(nc.sbuf_tensor("qz0", [128, S], BF16))
    qz1 = stk4.enter_context(nc.sbuf_tensor("qz1", [128, S], BF16))
    qz2 = stk4.enter_context(nc.sbuf_tensor("qz2", [128, S], BF16))
    qz3 = stk4.enter_context(nc.sbuf_tensor("qz3", [128, S], BF16))
    ao0 = stk4.enter_context(nc.sbuf_tensor("ao0", [128, S], BF16))
    ao1 = stk4.enter_context(nc.sbuf_tensor("ao1", [128, S], BF16))
    if True:
        p = plan()
        ktt = [ktt0, ktt1]
        qtt = [qtt0, qtt1]
        vtt = [vt0, vt1]
        aot = [ao0, ao1]
        probs = [pr0, pr1, pr2, pr3, pr4, pr5]
        rbxb = [rbx0, rbx1]
        rcpb = [rcp0, rcp1]
        ld_t = {}
        core_done_pe = {}
        ao_st = {}
        norm_ve = [0]
        score_free = {0: 0, 1: 0, 2: 0}
        probs_free_pe = [0] * 6
        pv_free_ve = [0]
        kg = [0]
        it_ctr = [0]
        qz_ve = {}
        qz_pe_free = [0, 0]
        rcp_gp = [0, 0]
        mul_gp = [0, 0]

        def emit_loads(t):
            tb = t % 2
            wld = [("ST", qkv_ld_gate[t])]
            if t >= 2:
                wld.append(("PE", core_done_pe[t - 2]))
            p.emit("sync",
                   lambda E, t=t, tb=tb: E.dma_start(
                       out=ktt[tb][:, :], in_=AP(kt_h, t * S, [[NDP * S, 128], [1, S]])),
                   wait=wld, inc=[("LD", 16)])
            p.emit("sync",
                   lambda E, t=t, tb=tb: E.dma_start(
                       out=qtt[tb][:, :], in_=AP(qt_h, t * S, [[NDP * S, 128], [1, S]])),
                   inc=[("LD", 16)])
            p.emit("sync",
                   lambda E, t=t, tb=tb: E.dma_start(
                       out=vtt[tb][:, :], in_=AP(v_h, t * 4160, [[NDP * 4160, 128], [1, 4160]])),
                   inc=[("LD", 16)])
            ld_t[t] = p.cnt("LD")

        def emit_qz(t):
            tb = t % 2
            qza = [qz0, qz1] if tb == 0 else [qz2, qz3]
            wqz = [("LD", ld_t[t])]
            if t >= 2:
                wqz.append(("PE", qz_pe_free[tb]))
            p.emit("vector", lambda E, tb=tb, qza=qza: E.tensor_copy(qza[0][0:64, :],
                                                                    qtt[tb][0:64, :]),
                   wait=wqz, inc=["VE"])
            p.emit("vector", lambda E, qza=qza: E.memset(qza[0][64:128, :], 0.0), inc=["VE"])
            p.emit("vector", lambda E, qza=qza: E.memset(qza[1][0:64, :], 0.0), inc=["VE"])
            p.emit("vector", lambda E, tb=tb, qza=qza: E.tensor_copy(qza[1][64:128, :],
                                                                     qtt[tb][64:128, :]),
                   inc=["VE"])
            qz_ve[t] = p.cnt("VE")

        emit_loads(0)
        emit_qz(0)
        for t in range(NDP):
            tb = t % 2
            qza = [qz0, qz1] if tb == 0 else [qz2, qz3]
            for hh in range(2):
                for qc in range(4):
                    it = it_ctr[0]
                    it_ctr[0] += 1
                    ip = it % 2
                    avb = [avst0, avst1][ip]
                    exp_at = {}

                    def emit_av(k, tb=tb, hh=hh):
                        wav = [("SC", exp_at[k])]
                        if k == 0:
                            wav.append(("VE", pv_free_ve[0]))
                        for half in range(2):
                            pav = [pA, pB][half]
                            p.emit("tensor",
                                   lambda E, tb=tb, hh=hh, k=k, pav=pav:
                                   E.matmul(pav[0:65, :],
                                            vtt[tb][:, k * 130 + hh * 65:k * 130 + hh * 65 + 65],
                                            probs[k % 6][:, half * 512:(half + 1) * 512],
                                            start=(k == 0), stop=(k == 31)),
                                   wait=wav if half == 0 else [], inc=["PE"])
                        probs_free_pe[k % 6] = p.cnt("PE")

                    for k in range(32):
                        reg = kg[0] % 3
                        kg[0] += 1
                        sc = [pE, pF, pG][reg]
                        wt = [("SC", score_free[reg])]
                        if k == 0:
                            wt += [("LD", ld_t[t]), ("VE", qz_ve[t])]
                        qz = qza[hh]
                        for half in range(2):
                            p.emit("tensor",
                                   lambda E, tb=tb, qc=qc, k=k, sc=sc, half=half, qz=qz:
                                   E.matmul(sc[:, half * 512:(half + 1) * 512],
                                            ktt[tb][:, k * 128:(k + 1) * 128],
                                            qz[:, qc * 1024 + half * 512:
                                               qc * 1024 + half * 512 + 512],
                                            start=True, stop=True),
                                   wait=wt if half == 0 else [], inc=["PE"])
                        pe_s = p.cnt("PE")
                        wex = [("PE", pe_s), ("PE", probs_free_pe[k % 6])]
                        p.emit("scalar",
                               lambda E, k=k, sc=sc: E.activation(probs[k % 6][:, :], sc[:, :],
                                                                  AF.Exp, scale=INV_SCALE),
                               wait=wex, inc=["SC"])
                        score_free[reg] = p.cnt("SC")
                        exp_at[k] = p.cnt("SC")
                        if k >= 1:
                            emit_av(k - 1)
                    emit_av(31)
                    pe_av = p.cnt("PE")
                    p.emit("vector",
                           lambda E, avb=avb: E.tensor_copy(avb[:, 0:512], pA[0:65, :]),
                           wait=[("PE", pe_av)], inc=["VE"])
                    p.emit("vector",
                           lambda E, avb=avb: E.tensor_copy(avb[:, 512:1024], pB[0:65, :]),
                           inc=["VE"])
                    pv_free_ve[0] = p.cnt("VE")
                    p.emit("vector",
                           lambda E, avb=avb, ip=ip: E.reciprocal(rcpb[ip][0:1, :],
                                                                  avb[64:65, :]),
                           wait=[("GP", rcp_gp[ip])], inc=["VE"])
                    ve_rc = p.cnt("VE")
                    p.emit("gpsimd",
                           lambda E, ip=ip: E.dma_start(
                               out=rbxb[ip][:, :],
                               in_=AP(rcpb[ip], 0, [[1024, 1], [0, 64], [1, 1024]])),
                           wait=[("VE", ve_rc)], inc=[("GP", 16)])
                    rcp_gp[ip] = p.cnt("GP")
                    mul_gp[ip] = p.cnt("GP")
                    wao = ([("ST", ao_st[t - 2])]
                           if (t >= 2 and hh == 0 and qc == 0) else [])
                    p.emit("vector",
                           lambda E, avb=avb, ip=ip, tb=tb, hh=hh, qc=qc: E.tensor_mul(
                               aot[tb][hh * 64:(hh + 1) * 64, qc * 1024:qc * 1024 + 512],
                               avb[0:64, 0:512], rbxb[ip][:, 0:512]),
                           wait=[("GP", mul_gp[ip])] + wao, inc=["VE"])
                    p.emit("vector",
                           lambda E, avb=avb, ip=ip, tb=tb, hh=hh, qc=qc: E.tensor_mul(
                               aot[tb][hh * 64:(hh + 1) * 64,
                                       qc * 1024 + 512:qc * 1024 + 1024],
                               avb[0:64, 512:1024], rbxb[ip][:, 512:1024]),
                           inc=["VE"])
                    norm_ve[0] = p.cnt("VE")
                    if hh == 0 and qc == 0 and t + 1 < NDP:
                        emit_loads(t + 1)
                    if hh == 1 and qc == 0 and t + 1 < NDP:
                        emit_qz(t + 1)
            core_done_pe[t] = p.cnt("PE")
            qz_pe_free[tb] = p.cnt("PE")
            p.emit("sync",
                   lambda E, t=t, tb=tb: E.dma_start(
                       out=AP(ao_h, t * S, [[NDP * S, 128], [1, S]]), in_=aot[tb][:, :]),
                   wait=[("VE", norm_ve[0])], inc=[("ST", 16)])
            ao_st[t] = p.cnt("ST")
            p.emit("gpsimd", lambda E: E.nop(),
                   wait=[("ST", p.cnt("ST")), ("OUT", p.cnt("OUT"))])
        ao_st_all[0] = p.cnt("ST")
        p.run("core")
        stk4.close()

    if debug_stage == "ao":
        return _finish_debug(nc, plan, counts, out_t, ao_h, BF16, NDP * S)
    if debug_stage == "x1ac":
        return _finish_debug(nc, plan, counts, out_t, x1t_h, F32, NDP * S)

    stk56 = ExitStack()
    _w1qb = stk56.enter_context(nc.sbuf_tensor("w1qb", [128, NDP * DFF], BF16))
    _w2qb = stk56.enter_context(nc.sbuf_tensor("w2qb", [128, NFT * D], BF16))
    stk5 = ExitStack()
    wo16 = stk5.enter_context(nc.sbuf_tensor("wo16", [128, NDP * D], BF16))
    aoc0 = stk5.enter_context(nc.sbuf_tensor("aoc0", [128, NDP * SCH], BF16))
    aoc1 = stk5.enter_context(nc.sbuf_tensor("aoc1", [128, NDP * SCH], BF16))
    ox0 = stk5.enter_context(nc.sbuf_tensor("ox0", [128, NDP * SCH], F32))
    ox1 = stk5.enter_context(nc.sbuf_tensor("ox1", [128, NDP * SCH], F32))
    oet0 = stk5.enter_context(nc.sbuf_tensor("oet0", [128, 512], F32))
    oet1 = stk5.enter_context(nc.sbuf_tensor("oet1", [128, 512], F32))
    if True:
        p = plan()
        p.emit("gpsimd",
               lambda E: E.dma_start(out=wo16[:, :],
                                     in_=AP(wins["wo"], 0, [[D, 128], [128 * D, NDP], [1, D]])),
               inc=[("LW", 16)])
        ld_wo = p.cnt("LW")
        wq_pref = [None]
        aoc = [aoc0, aoc1]
        oxc = [ox0, ox1]
        oet = [oet0, oet1]
        st_m = {}
        add_ve = {}
        ev_sc = {}
        ev_last = [0, 0]
        add_last = [0, 0]
        ld_ao = {}
        ld_ox = {}

        def emit_oload(c):
            cb = c % 2
            w = [("ST", st_m[c - 2])] if c >= 2 else []
            p.emit("sync",
                   lambda E, c=c, cb=cb: E.dma_start(
                       out=oxc[cb][:, :],
                       in_=AP(x1t_h, c * SCH, [[NDP * S, 128], [S, NDP], [1, SCH]])),
                   wait=w, inc=[("LA", 16)])
            p.emit("sync",
                   lambda E, c=c, cb=cb: E.dma_start(
                       out=aoc[cb][:, :],
                       in_=AP(ao_h, c * SCH, [[NDP * S, 128], [S, NDP], [1, SCH]])),
                   wait=[("ST", ao_st_all[0])] if c < 2 else [], inc=[("LA", 16)])
            ld_ao[c] = p.cnt("LA")
            ld_ox[c] = p.cnt("LA")

        emit_oload(0)
        emit_oload(1)
        p.emit("gpsimd", lambda E: E.dma_start(out=_w1qb[:, :], in_=w1q_h.ap()),
               wait=[("ST", wq_st_mark[0]), ("LA", ld_ao[1])], inc=[("LD2", 16)])
        p.emit("gpsimd", lambda E: E.dma_start(out=_w2qb[:, :], in_=w2q_h.ap()),
               inc=[("LD2", 16)])
        ld_wq2 = p.cnt("LD2")
        for c in range(NCH):
            cb = c % 2
            for dp in range(NDP):
                pp = pA if dp % 2 == 0 else pB
                wt = [("LA", ld_ao[c] if c >= 2 else ld_ao[1]),
                      ("SC", ev_last[dp % 2])]
                if c == 0 and dp == 0:
                    wt.append(("LW", ld_wo))
                for kp in range(NDP):
                    p.emit("tensor",
                           lambda E, dp=dp, kp=kp, pp=pp, cb=cb:
                           E.matmul(pp[:, :],
                                    wo16[:, kp * D + dp * 128:kp * D + dp * 128 + 128],
                                    aoc[cb][:, kp * SCH:(kp + 1) * SCH],
                                    start=(kp == 0), stop=(kp == NDP - 1)),
                           wait=wt if kp == 0 else [], inc=["PE"])
                pe_o = p.cnt("PE")
                eb = oet[dp % 2]
                p.emit("scalar",
                       lambda E, dp=dp, pp=pp, eb=eb:
                       E.activation(eb[:, :], pp[:, :], AF.Identity, bias=boT[:, dp:dp + 1]),
                       wait=[("PE", pe_o), ("VE", add_last[dp % 2])],
                       inc=["SC"])
                ev_last[dp % 2] = p.cnt("SC")
                ev_sc[dp] = p.cnt("SC")
                p.emit("vector",
                       lambda E, dp=dp, cb=cb, eb=eb:
                       E.tensor_add(oxc[cb][:, dp * SCH:(dp + 1) * SCH],
                                    oxc[cb][:, dp * SCH:(dp + 1) * SCH], eb[:, :]),
                       wait=[("SC", ev_sc[dp])] +
                            ([("LA", ld_ox[c] if c >= 2 else ld_ao[1])]
                             if dp == 0 else []),
                       inc=["VE"])
                add_last[dp % 2] = p.cnt("VE")
                add_ve[dp] = p.cnt("VE")
            ve_x2 = p.cnt("VE")
            p.emit("scalar",
                   lambda E, c=c, cb=cb: E.dma_start(
                       out=AP(x2t_h, c * SCH, [[NDP * S, 128], [S, NDP], [1, SCH]]),
                       in_=oxc[cb][:, :]),
                   wait=[("VE", ve_x2)], inc=[("ST", 16)])
            st_m[c] = p.cnt("ST")
            if c + 2 < NCH:
                emit_oload(c + 2)
            p.emit("gpsimd", lambda E: E.nop(),
                   wait=[("ST", p.cnt("ST")), ("OUT", p.cnt("OUT"))])
        p.run("oproj")
        stk5.close()

    if debug_stage == "x2":
        return _finish_debug(nc, plan, counts, out_t, x2t_h, F32, NDP * S)

    stk6 = ExitStack()
    g_xt0 = stk6.enter_context(nc.sbuf_tensor("g_xt0", [128, NDP * SCH], F32))
    g_xt1 = stk6.enter_context(nc.sbuf_tensor("g_xt1", [128, NDP * SCH], F32))
    g_xb0 = stk6.enter_context(nc.sbuf_tensor("g_xb0", [128, SCH], BF16))
    g_xb1 = stk6.enter_context(nc.sbuf_tensor("g_xb1", [128, SCH], BF16))
    g_xq0 = stk6.enter_context(nc.sbuf_tensor("g_xq0", [128, SCH], BF16))
    g_xq1 = stk6.enter_context(nc.sbuf_tensor("g_xq1", [128, SCH], BF16))
    g_yb = stk6.enter_context(nc.sbuf_tensor("g_yb", [128, NDP * SCH], BF16))
    g_ht = stk6.enter_context(nc.sbuf_tensor("g_ht", [128, NFT * (SCH // 2)], BF16))
    g_et0 = stk6.enter_context(nc.sbuf_tensor("g_et0", [128, 512], F32))
    g_et1 = stk6.enter_context(nc.sbuf_tensor("g_et1", [128, 512], F32))
    g_tmp = stk6.enter_context(nc.sbuf_tensor("g_tmp", [128, 512], F32))
    g_onat = stk6.enter_context(nc.sbuf_tensor("g_onat", [128, 2 * D], F32))
    if True:
        w1q_sb[0] = _w1qb
        w2q_sb[0] = _w2qb
        p = plan()
        ld_wq = ld_wq2
        trev_marks = {}
        dacc_sc = {}
        bufs = dict(xtb=[g_xt0, g_xt1], xb=[g_xb0, g_xb1], xsq=[g_xq0, g_xq1],
                    yb=g_yb, ht=g_ht, tmp512=g_tmp, etmp=[g_et0, g_et1],
                    onat=g_onat, dbg_x2chk=(debug_stage == "x2chk"))
        ffn_phase(p, "x2t", "out", bufs, [("LD2", ld_wq)] * NFT, [("LD2", ld_wq)] * NDP)
        p.emit("gpsimd", lambda E: E.nop(), wait=[("OUT", p.cnt("OUT"))])
        p.run("ffn2")
        stk6.close()
        stk56.close()

    if debug_stage == "x2chk":
        return _finish_debug(nc, plan, counts, out_t, ao_h, BF16, NDP * S)

    ctx.close()
    return nc


def _finish_debug(nc, plan, counts, out_t, src, dt, width):
    w = min(width, S * D // 128)
    p = plan()
    if dt == F32:
        p.emit("sync",
               lambda E: E.dma_start(out=AP(out_t, 0, [[w, 128], [1, w]]),
                                     in_=AP(src, 0, [[width, 128], [1, w]])),
               inc=[("OUT", 16)])
    else:
        p.emit("gpsimd",
               lambda E: E.dma_start(out=AP(out_t, 0, [[w, 128], [1, w]]),
                                     in_=AP(src, 0, [[width, 128], [1, w]])),
               inc=[("OUT", 16)])
    p.emit("gpsimd" if dt != F32 else "sync", lambda E: E.nop(),
           wait=[("OUT", counts["OUT"])])
    p.run("dbg")
    return nc


_NC_CACHE = {}


def _get_nc():
    key = DEBUG_STAGE
    if key not in _NC_CACHE:
        _NC_CACHE[key] = build_program(debug_stage=DEBUG_STAGE)
    return _NC_CACHE[key]


def kernel(**inputs):
    x = np.ascontiguousarray(np.asarray(inputs["L0_attn_x"], dtype=np.float32))
    B = x.shape[0]
    names = {
        "w1": "L0_ffn_W1", "b1": "L0_ffn_b1", "w2": "L0_ffn_W2", "b2": "L0_ffn_b2",
        "wdq": "L0_attn_Wd_q_w", "bdq": "L0_attn_Wd_q_b",
        "wuq": "L0_attn_Wu_q_w", "buq": "L0_attn_Wu_q_b",
        "wqr": "L0_attn_Wq_r_w", "bqr": "L0_attn_Wq_r_b",
        "wkr": "L0_attn_Wk_r_w", "bkr": "L0_attn_Wk_r_b",
        "wdkv": "L0_attn_Wd_kv_w", "bdkv": "L0_attn_Wd_kv_b",
        "wuk": "L0_attn_Wu_k_w", "buk": "L0_attn_Wu_k_b",
        "wuv": "L0_attn_Wu_v_w", "buv": "L0_attn_Wu_v_b",
        "wo": "L0_attn_Wo_w", "bo": "L0_attn_Wo_b",
    }
    shared = {k: np.ascontiguousarray(np.asarray(inputs[v], dtype=np.float32))
              for k, v in names.items()}
    shared["eye32"] = np.eye(128, dtype=np.float32)
    psw = np.zeros((32, 32), dtype=np.float32)
    for i in range(16):
        psw[2 * i + 1, 2 * i] = -1.0
        psw[2 * i, 2 * i + 1] = 1.0
    shared["psw"] = psw

    nc = _get_nc()
    in_maps = [dict(shared, x=np.ascontiguousarray(x[b])) for b in range(B)]
    res = run_bass_kernel_spmd(nc, in_maps, core_ids=list(range(B)))
    out = np.stack([np.asarray(res.results[b]["out"]) for b in range(B)], axis=0)
    return out.astype(np.float32)
